# version 22
# baseline (speedup 1.0000x reference)
import math
import contextlib
import numpy as np
import ml_dtypes
import concourse.bass as bass
import concourse.mybir as mybir
from concourse.bass_utils import run_bass_kernel_spmd

F32 = mybir.dt.float32
BF16 = mybir.dt.bfloat16
AF = mybir.ActivationFunctionType
ALU = mybir.AluOpType
AX = mybir.AxisListType
bf16 = ml_dtypes.bfloat16

D = 1024
NH = 16
HD = 64
NKT = D // 128
S_FULL = 8192
M_FFN = 64
INNER = 16
NJ = M_FFN * INNER // 128
EPS_MP = 1e-4
EPS_LN = 1e-5
ISQ2 = 1.0 / math.sqrt(2.0)


def build(T=4096, ST=4, pair_groups=None):
    NT = T // 128
    assert NT % ST == 0 and NT >= 8
    NS = NT // ST
    NCH = ST * 128 // 512
    if pair_groups is None:
        pair_groups = [[0, 1], [2, 3], [4, 5], [6, 7]]
    nc = bass.Bass("TRN2", target_bir_lowering=False, debug=False)

    x_in = nc.dram_tensor("x", [T, D], F32, kind="ExternalInput").ap()
    w_d = [nc.dram_tensor(n, [D, D], F32, kind="ExternalInput").ap()
           for n in ("wq", "wk", "wv", "wo")]
    gain_d = nc.dram_tensor("gainb", [128, 1], F32, kind="ExternalInput").ap()
    win_d = nc.dram_tensor("win", [D, M_FFN], F32, kind="ExternalInput").ap()
    bin_d = nc.dram_tensor("binc", [M_FFN, 1], F32, kind="ExternalInput").ap()
    w1bd_d = nc.dram_tensor("w1bd", [M_FFN, M_FFN * INNER], F32, kind="ExternalInput").ap()
    b1f_d = nc.dram_tensor("b1f", [128, NJ], F32, kind="ExternalInput").ap()
    w2bd_d = nc.dram_tensor("w2bd", [M_FFN * INNER, M_FFN], F32, kind="ExternalInput").ap()
    b2_d = nc.dram_tensor("b2c", [M_FFN, 1], F32, kind="ExternalInput").ap()
    wout_d = nc.dram_tensor("wout", [M_FFN, D], F32, kind="ExternalInput").ap()
    bout_d = nc.dram_tensor("boutr", [1, D], F32, kind="ExternalInput").ap()
    ident_d = nc.dram_tensor("ident", [128, 128], BF16, kind="ExternalInput").ap()
    eyes_d = nc.dram_tensor("eyes", [128, 128], F32, kind="ExternalInput").ap()
    csts_d = nc.dram_tensor("csts", [128, 2], F32, kind="ExternalInput").ap()
    out_d = nc.dram_tensor("out", [T, D], F32, kind="ExternalOutput").ap()
    cc_in = nc.dram_tensor("cc_in", [HD, NH * HD], F32)
    cc_out = nc.dram_tensor("cc_out", [HD, NH * HD], F32)
    cc_in2 = nc.dram_tensor("cc_in2", [HD, NH * HD], F32)
    cc_out2 = nc.dram_tensor("cc_out2", [HD, NH * HD], F32)

    ctx = contextlib.ExitStack()
    with ctx:
        sbt = lambda name, shape, dt: ctx.enter_context(nc.sbuf_tensor(name, shape, dt))
        pst = lambda name, shape, dt: ctx.enter_context(nc.psum_tensor(name, shape, dt))

        lnx_fm = sbt("lnx_fm", [128, NKT * T], BF16)
        wvN = sbt("wvN", [128, NKT * D], BF16)
        woT = sbt("woT", [128, NKT * D], BF16)
        MT = sbt("MT", [128, NKT * D], BF16)
        idents = sbt("idents_s", [128, 128], BF16)
        eyes = sbt("eyes_s", [128, 128], F32)
        csts = sbt("csts_s", [128, 2], F32)
        gainb = sbt("gainb_s", [128, 1], F32)
        NXB = 2
        xin = [sbt(f"xin{i}", [128, D], F32) for i in range(NXB)]
        o_sb = [sbt(f"o_sb{i}", [128, D], F32) for i in range(2)]
        lnx_tm = [sbt(f"lnx_tm{i}", [128, D], BF16) for i in range(2)]
        stats = sbt("stats", [128, 12], F32)
        mva = sbt("mva", [128, 2 * (4096 // 128)], F32)
        rstda = sbt("rstda", [128, 4096 // 128], F32)
        nba = sbt("nba", [128, 4096 // 128], F32)
        ssqqk = sbt("ssqqk", [128, 2 * NH], F32)
        lnt = sbt("lnt", [128, 2 * NH], F32)
        nrm = sbt("nrm", [128, 2 * NH], F32)
        aab = sbt("aab", [128, 2 * NH], F32)
        csc = sbt("cscl", [128, NH], F32)
        wssq = sbt("wssq", [128, NKT], F32)
        wsc = sbt("wsc", [128, 4 * NKT], F32)
        wlnt = sbt("wlnt", [128, NKT], F32)
        wnrm = sbt("wnrm", [128, NKT], F32)
        wab = sbt("wab", [128, NKT], F32)
        nmx = sbt("nmx", [HD, NH], F32)
        sums = sbt("sums", [HD, NH], F32)
        rsum = sbt("rsum", [HD, NH], F32)
        attn_b = sbt("attn_b", [HD, NH * HD], BF16)
        attnT = sbt("attnT", [128, NKT * HD], BF16)
        win_b = sbt("win_b", [128, NKT * M_FFN], BF16)
        w1bd_b = sbt("w1bd_b", [M_FFN, M_FFN * INNER], BF16)
        w2bd_b = sbt("w2bd_b", [128, NJ * M_FFN], BF16)
        wout_a = sbt("wout_a", [65, D], BF16)
        binc = sbt("binc_s", [M_FFN, 1], F32)
        b2c = sbt("b2c_s", [M_FFN, 1], F32)
        b1f = sbt("b1f_s", [128, NJ], F32)
        s_sb = sbt("s_sb", [M_FFN, ST * 128], BF16)
        gT_sb = sbt("gT_sb", [65, ST * 128], BF16)
        H_sb = [sbt(f"H_sb{i}", [128, ST * 128], BF16) for i in range(2)]
        sco = o_sb[0][0:HD, :]
        attn_e = o_sb[1][0:HD, :]

        ctx_qk = contextlib.ExitStack()
        wqT = ctx_qk.enter_context(nc.sbuf_tensor("wqT", [128, NKT * D], BF16, side="right"))
        wkT = ctx_qk.enter_context(nc.sbuf_tensor("wkT", [128, NKT * D], BF16, side="right"))

        ctx_sc = contextlib.ExitStack()
        wst = [ctx_sc.enter_context(nc.sbuf_tensor(f"wst{i}", [128, D], F32)) for i in range(2)]
        wstB = [ctx_sc.enter_context(nc.sbuf_tensor(f"wstB{i}", [128, D], F32)) for i in range(2)]
        qbf = [ctx_sc.enter_context(nc.sbuf_tensor(f"qbf{i}", [128, D], BF16)) for i in range(2)]
        wbf = qbf
        kbf = [ctx_sc.enter_context(nc.sbuf_tensor(f"kbf{i}", [128, D], BF16)) for i in range(2)]
        ktl = [ctx_sc.enter_context(nc.sbuf_tensor(f"ktl{i}", [128, D], BF16)) for i in range(2)]
        qsq = wstB[0]
        ksq = wstB[1]

        ptr = pst("ptr", [128, 2 * 1024], BF16)
        psc = pst("psc", [128, NH * HD], F32)
        ctx_ptw = contextlib.ExitStack()
        ptw = ctx_ptw.enter_context(nc.psum_tensor("ptw", [128, 2 * 1024], BF16))

        sems = {}

        def sem(name):
            if name not in sems:
                sems[name] = [ctx.enter_context(nc.semaphore(name)), 0]
            return sems[name]

        W = {}

        def mark(label, sname, instr, k=1):
            s = sem(sname)
            instr.then_inc(s[0], k)
            s[1] += k
            if label in W:
                assert W[label] == (s[0], s[1]), (label, W[label], s[1])
            W[label] = (s[0], s[1])

        def promise(label, sname, future_val):
            s = sem(sname)
            W[label] = (s[0], future_val)

        def wait(eng, label):
            h, v = W[label]
            eng.wait_ge(h, v)

        scalq = []

        def reserve(label):
            s = sem("s_sq")
            s[1] += 1
            W[label] = (s[0], s[1])

        def scal_defer(label, fn):
            reserve(label)
            scalq.append((label, fn))

        def scal_defer_raw(fn):
            scalq.append((None, fn))

        def inc_to(label, instr):
            h, v = W[label]
            instr.then_inc(h, 1)

        def scal_flush():
            items = list(scalq)
            scalq.clear()

            @block.scalar
            def _(scalar):
                for label, fn in items:
                    fn(scalar, label)

        edge_n = [0]

        def edge(eng, sname, instr):
            edge_n[0] += 1
            lbl = f"_e{edge_n[0]}"
            mark(lbl, sname, instr)
            wait(eng, lbl)

        for nme in ("s_wl0", "s_wl1", "s_wlB0", "s_wlB1", "s_cl", "s_pe", "s_v",
                    "s_a", "s_g", "s_gc", "s_cc", "s_cc2", "s_cb1", "s_cb2",
                    "s_st0", "s_st1", "s_sq"):
            sem(nme)
        s_xl = [sem(f"s_xl{i}") for i in range(NXB)]
        s_xf = [sem(f"s_xf{i}") for i in range(NXB)]
        s_x2l = [sem(f"s_x2l{i}") for i in range(NXB)]
        s_x2f = [sem(f"s_x2f{i}") for i in range(NXB)]

        block = ctx.enter_context(nc.Block())

        @block.sync
        def _(sync):
            i = sync.dma_start(out=idents[:], in_=ident_d)
            mark("c0", "s_cl", i, 16)
            i = sync.dma_start(out=eyes[:], in_=eyes_d)
            mark("c1", "s_cl", i, 16)
            i = sync.dma_start(out=csts[:], in_=csts_d)
            mark("c2", "s_cl", i, 16)
            i = sync.dma_start(out=gainb[:], in_=gain_d)
            mark("consts", "s_cl", i, 16)
            for t in range(NT):
                b = t % NXB
                if t >= NXB:
                    sync.wait_ge(s_xf[b][0], t // NXB)
                i = sync.dma_start(out=xin[b][:], in_=x_in[t * 128:(t + 1) * 128, :])
                mark(f"xload_{t}", f"s_xl{b}", i, 16)

        def wp_pass1_pool(w, stage_guard, prev_guards=None):
            base_v1 = sems["s_v"][1]
            for j in range(NKT):
                promise(f"wrs_{w}_{j}", "s_v", base_v1 + j + 1)

            @block.gpsimd
            def _(gpsimd):
                for jj in (0, 1):
                    if stage_guard is not None:
                        wait(gpsimd, stage_guard[jj])
                    i = gpsimd.dma_start(out=wst[jj][:],
                                         in_=w_d[w][jj * 128:(jj + 1) * 128, :])
                    mark(f"wld1_{w}_{jj}", f"s_wl{jj}", i, 16)
                for j in range(NKT):
                    sl = j % 2
                    wait(gpsimd, f"wld1_{w}_{j}")
                    if j >= 2:
                        wait(gpsimd, f"wrs_{w}_{j - 2}")
                    elif prev_guards is not None:
                        wait(gpsimd, prev_guards[j])
                    gpsimd.tensor_tensor(out=o_sb[sl][:], in0=wst[sl][:],
                                         in1=wst[sl][:], op=ALU.mult)
                    i = gpsimd.drain()
                    mark(f"psq_{w}_{j}", "s_gc", i)
                    if j + 2 < NKT:
                        i = gpsimd.dma_start(out=wst[sl][:],
                                             in_=w_d[w][(j + 2) * 128:(j + 3) * 128, :])
                        mark(f"wld1_{w}_{j + 2}", f"s_wl{sl}", i, 16)

            @block.vector
            def _(vector):
                for j in range(NKT):
                    wait(vector, f"psq_{w}_{j}")
                    i = vector.tensor_reduce(out=wssq[:, j:j + 1], in_=o_sb[j % 2][:],
                                             axis=AX.X, op=ALU.add)
                    mark(f"wrs_{w}_{j}", "s_v", i)
                wait(vector, f"wrs_{w}_{NKT - 1}")
                i = vector.tensor_scalar(out=wssq[:], in0=wssq[:], scalar1=1.0 / D,
                                         scalar2=None, op0=ALU.mult)
                mark(f"wrsc_{w}", "s_v", i)

        def wp_norm(w):
            def fn_wn0(scalar, label):
                wait(scalar, "consts")
                wait(scalar, f"wrsc_{w}")
                i = scalar.activation(out=wnrm[:], in_=wssq[:], func=AF.Sqrt,
                                      scale=1.0, bias=csts[:, 0:1])
                inc_to(label, i)
            scal_defer(f"wnorm_{w}", fn_wn0)

            @block.vector
            def _(vector):
                wait(vector, f"wnorm_{w}")
                vector.tensor_scalar(out=wab[:], in0=wnrm[:],
                                     scalar1=EPS_MP, scalar2=None, op0=ALU.add)
                vector.drain()
                vector.reciprocal(out=wlnt[:], in_=wab[:])
                vector.drain()
                i = vector.scalar_tensor_tensor(
                    out=wsc[:, w * NKT:(w + 1) * NKT], in0=wlnt[:],
                    scalar=1.0 / 32.0, in1=gainb[:].broadcast_to([128, NKT]),
                    op0=ALU.mult, op1=ALU.mult)
                mark(f"wsc_{w}", "s_v", i)

        def wp_pass2_vec(w, kind, dest, stage_guard, mt_guard_w=None):
            for j in range(NKT):
                sl = j % 2
                s = sem(f"s_wlB{sl}")
                s[1] += 16
                W[f"wld2_{w}_{j}"] = (s[0], s[1])

            def fn_ld01(scalar, _label, w=w):
                for jj in (0, 1):
                    i = scalar.dma_start(out=wstB[jj][:],
                                         in_=w_d[w][jj * 128:(jj + 1) * 128, :])
                    i.then_inc(W[f"wld2_{w}_{jj}"][0], 16)
            scal_defer_raw(fn_ld01)

            def mk_cast(j, w=w):
                def fn(scalar, label):
                    if j == 0:
                        wait(scalar, f"wsc_{w}")
                    wait(scalar, f"wld2_{w}_{j}")
                    tgt = (dest[:, j * D:(j + 1) * D] if kind == "N"
                           else MT[:, j * D:(j + 1) * D])
                    if kind != "N" and mt_guard_w is not None:
                        wait(scalar, f"wtr_{mt_guard_w}_{j}")
                    i = scalar.activation(
                        out=tgt, in_=wstB[j % 2][:], func=AF.Copy,
                        scale=wsc[:, w * NKT + j: w * NKT + j + 1])
                    inc_to(label, i)
                scal_defer(f"wcast_{w}_{j}", fn)
                if j + 2 < NKT:
                    def fnl(scalar, _label, j=j):
                        wait(scalar, f"wcast_{w}_{j}")
                        i = scalar.dma_start(out=wstB[j % 2][:],
                                             in_=w_d[w][(j + 2) * 128:(j + 3) * 128, :])
                        i.then_inc(W[f"wld2_{w}_{j + 2}"][0], 16)
                    scal_defer_raw(fnl)
            for j in range(NKT):
                mk_cast(j)

        def wp_post_transpose(w, dest, ptr_guard, pbuf=None):
            base_pe = sems["s_pe"][1]
            for j in range(NKT):
                promise(f"wtr_{w}_{j}", "s_pe", base_pe + j + 1)

            pb2 = ptr if pbuf is None else pbuf

            def mk_ev(j, w=w, dest=dest, pb=pb2):
                def fn(scalar, label):
                    half = j % 2
                    wait(scalar, f"wtr_{w}_{j}")
                    i = scalar.activation(
                        out=dest[:].rearrange("p (kk o) -> p kk o", o=D)[:, :, j * 128:(j + 1) * 128],
                        in_=pb[:, half * 1024:(half + 1) * 1024]
                        .rearrange("p (kk e) -> p kk e", e=128),
                        func=AF.Copy, scale=1.0)
                    inc_to(label, i)
                scal_defer(f"wtev_{w}_{j}", fn)
            for j in range(NKT):
                mk_ev(j)

            @block.tensor
            def _(tensor):
                wait(tensor, f"wcast_{w}_{NKT - 1}")
                for j in range(NKT):
                    half = j % 2
                    if j >= 2:
                        wait(tensor, f"wtev_{w}_{j - 2}")
                    elif ptr_guard is not None:
                        wait(tensor, ptr_guard)
                    pb = ptr if pbuf is None else pbuf
                    for kk in range(NKT):
                        i = tensor.transpose(
                            out=pb[:, half * 1024 + kk * 128: half * 1024 + (kk + 1) * 128],
                            in_=MT[:, j * D + kk * 128: j * D + (kk + 1) * 128],
                            identity=idents[:])
                    mark(f"wtr_{w}_{j}", "s_pe", i)

        def emit_weight_startup():
            wp_pass1_pool(0, None)
            wp_norm(0)
            wp_pass1_pool(1, None, prev_guards=["wrs_0_6", "wrs_0_7"])
            wp_pass2_vec(0, "T_viaMT", wqT, None)
            wp_post_transpose(0, wqT, None, pbuf=ptw)
            wp_norm(1)
            wp_pass2_vec(1, "T_viaMT", wkT, None, mt_guard_w=0)
            wp_post_transpose(1, wkT, "wtev_0_7", pbuf=ptw)

        ctx_ptw_closed = [False]
        pqv = None
        pk = None

        def tile_front(t):
            b = t % NXB
            lb = t % 2

            @block.vector
            def _(vector, t=t, b=b):
                h, v = W[f"xload_{t}"]
                vector.wait_ge(h, v)
                vector.bn_stats(stats[:, 0:6], xin[b][:, 0:512])
                i = vector.bn_stats(stats[:, 6:12], xin[b][:, 512:1024])
                edge(vector, "s_v", i)
                i = vector.bn_aggr(mva[:, 2 * t:2 * t + 2], stats[:])
                mark(f"mv_{t}", "s_v", i)

            def fn_srt(scalar, label, t=t):
                if t == 0:
                    wait(scalar, "consts")
                wait(scalar, f"mv_{t}")
                i = scalar.activation(out=mva[:, 2 * t + 1:2 * t + 2],
                                      in_=mva[:, 2 * t + 1:2 * t + 2],
                                      func=AF.Sqrt, scale=1.0, bias=csts[:, 1:2])
                inc_to(label, i)
            scal_defer(f"srt_{t}", fn_srt)

            @block.vector
            def _(vector, t=t, b=b, lb=lb):
                wait(vector, f"srt_{t}")
                i = vector.reciprocal(out=rstda[:, t:t + 1], in_=mva[:, 2 * t + 1:2 * t + 2])
                mark(f"rstd_{t}", "s_v", i)
                wait(vector, f"rstd_{t}")
                i = vector.scalar_tensor_tensor(out=nba[:, t:t + 1],
                                                in0=rstda[:, t:t + 1], scalar=-1.0,
                                                in1=mva[:, 2 * t:2 * t + 1],
                                                op0=ALU.mult, op1=ALU.mult)
                edge(vector, "s_v", i)
                if t >= 2:
                    wait(vector, f"tr_{t - 2}")
                i = vector.tensor_scalar(out=lnx_tm[lb][:], in0=xin[b][:],
                                         scalar1=rstda[:, t:t + 1],
                                         scalar2=nba[:, t:t + 1],
                                         op0=ALU.mult, op1=ALU.add)
                mark(f"lnx_{t}", "s_v", i)
                i2 = vector.drain()
                s = s_xf[b]
                i2.then_inc(s[0], 1)
                s[1] += 1

        def tile_tr(t):
            lb = t % 2

            @block.tensor
            def _(tensor, t=t, lb=lb):
                wait(tensor, f"lnx_{t}")
                if t >= 2:
                    wait(tensor, f"trev_{t - 2}")
                half = t % 2
                for kk in range(NKT):
                    i = tensor.transpose(
                        out=ptr[:, half * 1024 + kk * 128: half * 1024 + (kk + 1) * 128],
                        in_=lnx_tm[lb][:, kk * 128:(kk + 1) * 128],
                        identity=idents[:])
                mark(f"tr_{t}", "s_pe", i)

            def fn_trev(scalar, label, t=t):
                half = t % 2
                wait(scalar, f"tr_{t}")
                i = scalar.activation(
                    out=lnx_fm[:].rearrange("p (kk tok) -> p kk tok", tok=T)[:, :, t * 128:(t + 1) * 128],
                    in_=ptr[:, half * 1024:(half + 1) * 1024].rearrange("p (kk e) -> p kk e", e=128),
                    func=AF.Copy, scale=1.0)
                inc_to(label, i)
            scal_defer(f"trev_{t}", fn_trev)

        def tile_mm_a(t):
            lb = t % 2

            @block.tensor
            def _(tensor, t=t):
                wait(tensor, f"trev_{t}")
                if t == 0:
                    wait(tensor, "wtev_0_7")
                    wait(tensor, "wtev_1_7")
                if t >= 1:
                    wait(tensor, f"qbf_{t - 1}")
                for nch in range(2):
                    for kk in range(NKT):
                        i = tensor.matmul(
                            out=pqv[:, nch * 512:(nch + 1) * 512],
                            lhsT=lnx_fm[:, kk * T + t * 128: kk * T + (t + 1) * 128],
                            rhs=wqT[:, kk * D + nch * 512: kk * D + (nch + 1) * 512],
                            start=(kk == 0), stop=(kk == NKT - 1))
                mark(f"qmm_{t}", "s_pe", i)
                if t >= 1:
                    wait(tensor, f"kbf_{t - 1}")
                for nch in range(2):
                    for kk in range(NKT):
                        i = tensor.matmul(
                            out=pk[:, nch * 512:(nch + 1) * 512],
                            lhsT=lnx_fm[:, kk * T + t * 128: kk * T + (t + 1) * 128],
                            rhs=wkT[:, kk * D + nch * 512: kk * D + (nch + 1) * 512],
                            start=(kk == 0), stop=(kk == NKT - 1))
                mark(f"kmm_{t}", "s_pe", i)

            def fn_evac(scalar, label, t=t, lb=lb):
                wait(scalar, f"qmm_{t}")
                if t >= 2:
                    wait(scalar, f"scmm_{t - 2}")
                    wait(scalar, f"sq_{t - 2}")
                i = scalar.activation(out=qbf[lb][:], in_=pqv[:],
                                      func=AF.Copy, scale=0.125)
                inc_to(label, i)
            scal_defer(f"qbf_{t}", fn_evac)

            def fn_evack(scalar, label, t=t, lb=lb):
                wait(scalar, f"kmm_{t}")
                i = scalar.activation(out=kbf[lb][:], in_=pk[:],
                                      func=AF.Copy, scale=0.125)
                inc_to(label, i)
            scal_defer(f"kbf_{t}", fn_evack)

            @block.gpsimd
            def _(gpsimd, t=t, lb=lb):
                wait(gpsimd, f"qbf_{t}")
                if t >= 1:
                    wait(gpsimd, f"ssq_{t - 1}")
                gpsimd.tensor_tensor(out=qsq[:], in0=qbf[lb][:], in1=qbf[lb][:],
                                     op=ALU.mult)
                wait(gpsimd, f"kbf_{t}")
                i = gpsimd.tensor_tensor(out=ksq[:], in0=kbf[lb][:], in1=kbf[lb][:],
                                         op=ALU.mult)
                i2 = gpsimd.drain()
                mark(f"sq_{t}", "s_gc", i2)

        def tile_mm_b(t):
            lb = t % 2

            @block.vector
            def _(vector, t=t, lb=lb):
                wait(vector, f"sq_{t}")
                if t >= 1:
                    wait(vector, f"nrm_{t - 1}")
                vector.tensor_reduce(out=ssqqk[:, 0:NH],
                                     in_=qsq[:].rearrange("p (n h) -> p n h", h=HD),
                                     axis=AX.X, op=ALU.add)
                i = vector.tensor_reduce(out=ssqqk[:, NH:2 * NH],
                                         in_=ksq[:].rearrange("p (n h) -> p n h", h=HD),
                                         axis=AX.X, op=ALU.add)
                mark(f"ssq_{t}", "s_v", i)

            def fn_nrm(scalar, label, t=t):
                wait(scalar, f"ssq_{t}")
                i = scalar.activation(out=nrm[:], in_=ssqqk[:], func=AF.Sqrt,
                                      scale=1.0, bias=csts[:, 0:1])
                inc_to(label, i)
            scal_defer(f"nrm_{t}", fn_nrm)

            @block.vector
            def _(vector, t=t, lb=lb):
                wait(vector, f"nrm_{t}")
                i = vector.tensor_scalar(out=aab[:], in0=nrm[:], scalar1=EPS_MP,
                                         scalar2=None, op0=ALU.add)
                edge(vector, "s_v", i)
                i = vector.reciprocal(out=lnt[:], in_=aab[:])
                edge(vector, "s_v", i)
                i = vector.scalar_tensor_tensor(out=csc[:], in0=lnt[:, 0:NH],
                                                scalar=64.0 / math.sqrt(S_FULL),
                                                in1=lnt[:, NH:2 * NH],
                                                op0=ALU.mult, op1=ALU.mult)
                mark(f"csc_{t}", "s_v", i)
                wait(vector, f"csc_{t}")
                i = vector.tensor_tensor(
                    out=ktl[lb][:].rearrange("p (n h) -> p n h", h=HD),
                    in0=kbf[lb][:].rearrange("p (n h) -> p n h", h=HD),
                    in1=csc[:].unsqueeze(2).broadcast_to([128, NH, HD]),
                    op=ALU.mult)
                mark(f"ktl_{t}", "s_v", i)

        def tile_scores(t):
            lb = t % 2

            @block.tensor
            def _(tensor, t=t, lb=lb):
                wait(tensor, f"qbf_{t}")
                wait(tensor, f"ktl_{t}")
                if t == NT // 2:
                    wait(tensor, "sco1")
                for n in range(NH):
                    i = tensor.matmul(
                        out=psc[0:HD, n * HD:(n + 1) * HD],
                        lhsT=qbf[lb][:, n * HD:(n + 1) * HD],
                        rhs=ktl[lb][:, n * HD:(n + 1) * HD],
                        start=(t in (0, NT // 2)) and (n % 8 == 0),
                        stop=(t in (NT // 2 - 1, NT - 1)) and (n % 8 == 7),
                        skip_group_check=True)
                mark(f"scmm_{t}", "s_pe", i)

        def wp_piece_slots(w, p1_guard, kind, dest, mt_guard_w=None):
            def sq_piece(j):
                def em():
                    @block.gpsimd
                    def _(gpsimd):
                        if j == 0:
                            for jj in (0, 1):
                                if p1_guard is not None:
                                    wait(gpsimd, p1_guard[jj])
                                i = gpsimd.dma_start(
                                    out=wst[jj][:],
                                    in_=w_d[w][jj * 128:(jj + 1) * 128, :])
                                mark(f"wld1_{w}_{jj}", f"s_wl{jj}", i, 16)
                        wait(gpsimd, f"wld1_{w}_{j}")
                        if j >= 1:
                            wait(gpsimd, f"wrs_{w}_{j - 1}")
                        elif w == 2:
                            wait(gpsimd, "wrs_1_7")
                        gpsimd.tensor_tensor(out=o_sb[1][:], in0=wst[j % 2][:],
                                             in1=wst[j % 2][:], op=ALU.mult)
                        i = gpsimd.drain()
                        mark(f"psq_{w}_{j}", "s_gc", i)
                        if j + 2 < NKT:
                            i = gpsimd.dma_start(
                                out=wst[j % 2][:],
                                in_=w_d[w][(j + 2) * 128:(j + 3) * 128, :])
                            mark(f"wld1_{w}_{j + 2}", f"s_wl{j % 2}", i, 16)
                return em

            def norm_piece():
                def em():
                    @block.vector
                    def _(vector):
                        wait(vector, f"wrs_{w}_{NKT - 1}")
                        i = vector.tensor_scalar(out=wssq[:], in0=wssq[:],
                                                 scalar1=1.0 / D,
                                                 scalar2=None, op0=ALU.mult)
                        mark(f"wrsc_{w}", "s_v", i)

                    def fn_wn(scalar, label):
                        wait(scalar, f"wrsc_{w}")
                        i = scalar.activation(out=wnrm[:], in_=wssq[:],
                                              func=AF.Sqrt,
                                              scale=1.0, bias=csts[:, 0:1])
                        inc_to(label, i)
                    scal_defer(f"wnorm_{w}", fn_wn)

                    @block.vector
                    def _(vector):
                        wait(vector, f"wnorm_{w}")
                        vector.tensor_scalar(out=wab[:], in0=wnrm[:],
                                             scalar1=EPS_MP, scalar2=None,
                                             op0=ALU.add)
                        vector.drain()
                        vector.reciprocal(out=wlnt[:], in_=wab[:])
                        vector.drain()
                        i = vector.scalar_tensor_tensor(
                            out=wsc[:, w * NKT:(w + 1) * NKT], in0=wlnt[:],
                            scalar=1.0 / 32.0,
                            in1=gainb[:].broadcast_to([128, NKT]),
                            op0=ALU.mult, op1=ALU.mult)
                        mark(f"wsc_{w}", "s_v", i)
                return em

            def ld2_piece(j):
                def em():
                    @block.gpsimd
                    def _(gpsimd):
                        if j >= 2:
                            wait(gpsimd, f"wcast_{w}_{j - 2}")
                        else:
                            wait(gpsimd, f"wrs_{w}_{6 + j}")
                        i = gpsimd.dma_start(out=wst[j % 2][:],
                                             in_=w_d[w][j * 128:(j + 1) * 128, :])
                        mark(f"wld2_{w}_{j}", f"s_wl{j % 2}", i, 16)
                return em

            def cast_piece(j):
                def em():
                    def fn(scalar, label):
                        if j == 0:
                            wait(scalar, f"wsc_{w}")
                        wait(scalar, f"wld2_{w}_{j}")
                        tgt = (dest[:, j * D:(j + 1) * D] if kind == "N"
                               else MT[:, j * D:(j + 1) * D])
                        if kind != "N" and mt_guard_w is not None:
                            wait(scalar, f"wtr_{mt_guard_w}_{j}")
                        i = scalar.activation(
                            out=tgt, in_=wst[j % 2][:], func=AF.Copy,
                            scale=wsc[:, w * NKT + j: w * NKT + j + 1])
                        inc_to(label, i)
                    scal_defer(f"wcast_{w}_{j}", fn)
                return em

            def rd_piece(j):
                def em():
                    @block.vector
                    def _(vector):
                        wait(vector, f"psq_{w}_{j}")
                        i = vector.tensor_reduce(out=wssq[:, j:j + 1],
                                                 in_=o_sb[1][:],
                                                 axis=AX.X, op=ALU.add)
                        mark(f"wrs_{w}_{j}", "s_v", i)
                return em

            slots = [[sq_piece(0)]]
            for j in range(1, NKT):
                slots.append([rd_piece(j - 1), sq_piece(j)])
            slots.append([rd_piece(NKT - 1), norm_piece()])
            slots.append([ld2_piece(0), ld2_piece(1)])
            for j0 in range(0, NKT - 2, 2):
                slots.append([cast_piece(j0), ld2_piece(j0 + 2),
                              cast_piece(j0 + 1), ld2_piece(j0 + 3)])
            slots.append([cast_piece(NKT - 2), cast_piece(NKT - 1)])
            return slots

        wslots = (wp_piece_slots(2, None, "N", wvN)
                  + wp_piece_slots(3, ["wcast_2_6", "wcast_2_7"], "T_viaMT", woT,
                                   mt_guard_w=1))

        emit_weight_startup()
        ctx_ptw.close()
        pqv = pst("pqv", [128, D], F32)
        pk = pst("pk", [128, D], F32)
        tile_front(0)
        tile_front(1)
        tile_tr(0)
        tile_front(2)
        tile_tr(1)
        tile_front(3)
        tile_tr(2)
        for t in range(NT):
            if t + 4 < NT:
                tile_front(t + 4)
            if t >= 1:
                tile_mm_b(t - 1)
            if t >= 2:
                tile_scores(t - 2)
            tile_mm_a(t)
            if t + 3 < NT:
                tile_tr(t + 3)
            if 0 <= t - 2 < len(wslots):
                for em in wslots[t - 2]:
                    em()
            if t == NT // 2 + 1:
                @block.vector
                def _(vector):
                    wait(vector, f"scmm_{NT // 2 - 1}")
                    i = vector.tensor_copy(sco, psc[0:HD, :])
                    mark("sco1", "s_v", i)

                @block.sync
                def _(sync):
                    wait(sync, "sco1")
                    i = sync.dma_start(out=cc_in.ap(), in_=sco)
                    mark("cc1_in", "s_cc", i, 16)
            if t == min(29, NT - 3):
                @block.gpsimd
                def _(gpsimd):
                    wait(gpsimd, "cc1_in")
                    i = gpsimd.collective_compute(
                        "AllReduce", ALU.add, replica_groups=pair_groups,
                        ins=[cc_in.ap()], outs=[cc_out.ap()])
                    mark("cc1_done", "s_gc", i)
        for si in range(max(0, NT - 2), len(wslots)):
            for em in wslots[si]:
                em()
        tile_mm_b(NT - 1)
        tile_scores(NT - 2)
        tile_scores(NT - 1)
        wp_post_transpose(3, woT, f"trev_{NT - 1}")
        scal_flush()

        @block.gpsimd
        def _(gpsimd):
            wait(gpsimd, "wcast_3_6")
            i = gpsimd.dma_start(out=wst[0][:, 0:NKT * M_FFN]
                                 .rearrange("p (j m) -> p j m", m=M_FFN),
                                 in_=win_d.rearrange("(j p) m -> p j m", p=128))
            mark("win_loaded", "s_wl0", i, 16)
            wait(gpsimd, "wcast_3_7")
            i = gpsimd.dma_start(out=wst[1][0:M_FFN, :], in_=wout_d)
            mark("f0", "s_wl1", i, 16)
            i = gpsimd.dma_start(out=wst[1][64:65, :], in_=bout_d)
            mark("wout_loaded", "s_wl1", i, 16)
            i = gpsimd.dma_start(out=binc[:], in_=bin_d)
            mark("f1", "s_g", i, 16)
            i = gpsimd.dma_start(out=b2c[:], in_=b2_d)
            mark("f2", "s_g", i, 16)
            i = gpsimd.dma_start(out=b1f[:], in_=b1f_d)
            mark("ffnb_loaded", "s_g", i, 16)
            gpsimd.memset(gT_sb[64:65, :], 1.0)
            i = gpsimd.drain()
            mark("gones", "s_gc", i)

        @block.vector
        def _(vector):
            wait(vector, "win_loaded")
            vector.tensor_copy(win_b[:], wst[0][:, 0:NKT * M_FFN])
            wait(vector, "wout_loaded")
            vector.tensor_copy(wout_a[0:M_FFN, :], wst[1][0:M_FFN, :])
            i = vector.tensor_copy(wout_a[64:65, :], wst[1][64:65, :])
            mark("ffncast1", "s_v", i)

        @block.gpsimd
        def _(gpsimd):
            wait(gpsimd, "ffncast1")
            i = gpsimd.dma_start(out=wst[0][0:M_FFN, :], in_=w1bd_d)
            mark("w1bd_loaded", "s_wl0", i, 16)

        @block.vector
        def _(vector):
            wait(vector, "w1bd_loaded")
            i = vector.tensor_copy(w1bd_b[:], wst[0][0:M_FFN, :])
            mark("w1bd_cast", "s_v", i)

        @block.gpsimd
        def _(gpsimd):
            wait(gpsimd, "w1bd_cast")
            i = gpsimd.dma_start(
                out=wst[1][:, 0:NJ * M_FFN].rearrange("p (j m) -> p j m", m=M_FFN),
                in_=w2bd_d.rearrange("(j p) m -> p j m", p=128))
            mark("w2bd_loaded", "s_wl1", i, 16)

        @block.vector
        def _(vector):
            wait(vector, "w2bd_loaded")
            i = vector.tensor_copy(w2bd_b[:], wst[1][:, 0:NJ * M_FFN])
            mark("w2bd_cast", "s_v", i)

        @block.vector
        def _(vector):
            wait(vector, f"scmm_{NT - 1}")
            i = vector.tensor_copy(attn_e, psc[0:HD, :])
            mark("sco2", "s_v", i)

        @block.sync
        def _(sync):
            wait(sync, "sco2")
            i = sync.dma_start(out=cc_in2.ap(), in_=attn_e)
            mark("cc2_in", "s_cc2", i, 16)
            wait(sync, "cc1_done")
            i = sync.dma_start(out=sco, in_=cc_out.ap())
            mark("cc1_back", "s_cb1", i, 16)

        @block.gpsimd
        def _(gpsimd):
            wait(gpsimd, "cc2_in")
            i = gpsimd.collective_compute(
                "AllReduce", ALU.add, replica_groups=pair_groups,
                ins=[cc_in2.ap()], outs=[cc_out2.ap()])
            mark("cc2_done", "s_gc", i)

        @block.sync
        def _(sync):
            wait(sync, "cc2_done")
            i = sync.dma_start(out=attn_e, in_=cc_out2.ap())
            mark("cc2_back", "s_cb2", i, 16)

        @block.vector
        def _(vector):
            wait(vector, "cc1_back")
            wait(vector, "cc2_back")
            i = vector.tensor_tensor(out=sco, in0=sco, in1=attn_e, op=ALU.add)
            mark("scot", "s_v", i)

        @block.scalar
        def _(scalar):
            wait(scalar, "scot")
            i = scalar.activation(out=attn_e[:], in_=sco[:],
                                  func=AF.Exp, scale=1.0, bias=0.0)
            mark("attn_exp", "s_a", i)

        @block.vector
        def _(vector):
            wait(vector, "attn_exp")
            vector.tensor_reduce(out=sums[:], in_=attn_e.rearrange("p (n h) -> p n h", h=HD),
                                 axis=AX.X, op=ALU.add)
            vector.drain()
            vector.reciprocal(out=rsum[:], in_=sums[:])
            vector.drain()
            i = vector.tensor_tensor(
                out=attn_b[:].rearrange("p (n h) -> p n h", h=HD),
                in0=attn_e.rearrange("p (n h) -> p n h", h=HD),
                in1=rsum[:].unsqueeze(2).broadcast_to([HD, NH, HD]),
                op=ALU.mult)
            mark("attn_b", "s_v", i)

        @block.tensor
        def _(tensor):
            wait(tensor, "attn_b")
            wait(tensor, "wtev_3_7")
            for j in range(NKT):
                i = tensor.transpose(out=ptr[:, j * 128: j * 128 + 64],
                                     in_=attn_b[:, j * 128:(j + 1) * 128],
                                     identity=idents[0:64, 0:64])
            mark("attnT_tr", "s_pe", i)

        @block.vector
        def _(vector):
            wait(vector, "attnT_tr")
            i = vector.tensor_copy(
                attnT[:].rearrange("p (j h) -> p j h", h=HD),
                ptr[:, 0:1024].rearrange("p (j e) -> p j e", e=128)[:, :, 0:64])
            mark("attnT", "s_v", i)

        ctx_sc.close()
        A_sb = ctx.enter_context(nc.sbuf_tensor("A_sb", [128, NKT * D], BF16))

        def mk_aev(j):
            def fn(scalar, label):
                wait(scalar, f"Amm_{j}")
                scalar.activation(out=A_sb[0:64, j * D:(j + 1) * D], in_=pqv[0:64, :],
                                  func=AF.Copy, scale=1.0)
                i = scalar.activation(out=A_sb[64:128, j * D:(j + 1) * D], in_=pk[64:128, :],
                                      func=AF.Copy, scale=1.0)
                inc_to(label, i)
            scal_defer(f"Aev_{j}", fn)
        for j in range(NKT):
            mk_aev(j)

        @block.tensor
        def _(tensor):
            wait(tensor, "attnT")
            wait(tensor, "wcast_2_7")
            for j in range(NKT):
                if j >= 1:
                    wait(tensor, f"Aev_{j - 1}")
                for nch in range(2):
                    tensor.matmul(
                        out=pqv[0:64, nch * 512:(nch + 1) * 512],
                        lhsT=attnT[0:64, j * HD:(j + 1) * HD],
                        rhs=wvN[0:64, j * D + nch * 512: j * D + (nch + 1) * 512],
                        start=True, stop=True)
                    i = tensor.matmul(
                        out=pk[64:128, nch * 512:(nch + 1) * 512],
                        lhsT=attnT[64:128, j * HD:(j + 1) * HD],
                        rhs=wvN[64:128, j * D + nch * 512: j * D + (nch + 1) * 512],
                        start=True, stop=True)
                mark(f"Amm_{j}", "s_pe", i)

        base_v2 = sems["s_v"][1]
        for es in range(NKT):
            promise(f"MTev_{es}", "s_v", base_v2 + 2 * es + 2)

        @block.tensor
        def _(tensor):
            wait(tensor, f"Aev_{NKT - 1}")
            for es in range(NKT):
                if es >= 2:
                    wait(tensor, f"MTev_{es - 2}")
                pbank = pqv if es % 2 == 0 else pk
                for nch in range(2):
                    for kk in range(NKT):
                        i = tensor.matmul(
                            out=pbank[:, nch * 512:(nch + 1) * 512],
                            lhsT=A_sb[:, kk * D + es * 128: kk * D + (es + 1) * 128],
                            rhs=woT[:, kk * D + nch * 512: kk * D + (nch + 1) * 512],
                            start=(kk == 0), stop=(kk == NKT - 1))
                mark(f"MTmm_{es}", "s_pe", i)

        @block.vector
        def _(vector):
            for es in range(NKT):
                pbank = pqv if es % 2 == 0 else pk
                wait(vector, f"MTmm_{es}")
                vector.tensor_scalar(out=MT[:, es * D:(es + 1) * D], in0=pbank[:],
                                     scalar1=ISQ2, scalar2=None, op0=ALU.mult)
                s_v_ = sems["s_v"]
                i0 = vector.drain()
                i0.then_inc(s_v_[0], 1)
                s_v_[1] += 1
                i = vector.tensor_tensor(
                    out=MT[:, es * D + es * 128: es * D + (es + 1) * 128],
                    in0=MT[:, es * D + es * 128: es * D + (es + 1) * 128],
                    in1=eyes[:], op=ALU.add)
                mark(f"MTev_{es}", "s_v", i)

        ctx_qk.close()
        x2sb = [ctx.enter_context(nc.sbuf_tensor(f"x2sb{i}", [128, ST * D], F32, side="right"))
                for i in range(2)]
        xfA = [s_xf[i][1] for i in range(NXB)]

        def b_x2loads(s):
            @block.sync
            def _(sync, s=s):
                for ti in range(ST):
                    t = s * ST + ti
                    b = t % NXB
                    if t < NXB:
                        sync.wait_ge(s_xf[b][0], xfA[b])
                    else:
                        sync.wait_ge(s_x2f[b][0], t // NXB)
                    i = sync.dma_start(out=xin[b][:], in_=x_in[t * 128:(t + 1) * 128, :])
                    mark(f"x2load_{t}", f"s_x2l{b}", i, 16)

        def b_vo(t):
            s, ti = t // ST, t % ST
            pb_vo = pqv if t % 2 == 0 else psc

            @block.tensor
            def _(tensor, t=t, s=s, ti=ti, pb_vo=pb_vo):
                if t == 0:
                    wait(tensor, f"MTev_{NKT - 1}")
                if t >= 2:
                    wait(tensor, f"x2_{t - 2}")
                if t % 2 == 0 and t >= 6:
                    wait(tensor, f"ov_{t - 6}")
                if t % ST == 1 and t >= 5:
                    wait(tensor, f"Hev_{(t - 5) // ST}_7")
                for nch in range(2):
                    for kk in range(NKT):
                        i = tensor.matmul(
                            out=pb_vo[:, nch * 512:(nch + 1) * 512],
                            lhsT=lnx_fm[:, kk * T + t * 128: kk * T + (t + 1) * 128],
                            rhs=MT[:, kk * D + nch * 512: kk * D + (nch + 1) * 512],
                            start=(kk == 0), stop=(kk == NKT - 1))
                mark(f"vo_{t}", "s_pe", i)

        def b_chain(t):
            s, ti = t // ST, t % ST
            b = t % NXB
            lb = t % 2
            sslot = s % 2
            pb_vo = pqv if t % 2 == 0 else psc

            @block.vector
            def _(vector, t=t, ti=ti, s=s, b=b, sslot=sslot, pb_vo=pb_vo):
                wait(vector, f"vo_{t}")
                h, v = W[f"x2load_{t}"]
                vector.wait_ge(h, v)
                if s >= 2 and ti == 0:
                    wait(vector, f"stored_{(s - 2) * ST + ST - 1}")
                i = vector.tensor_tensor(out=x2sb[sslot][:, ti * D:(ti + 1) * D],
                                         in0=xin[b][:], in1=pb_vo[:], op=ALU.add)
                mark(f"x2_{t}", "s_v", i)
                wait(vector, f"x2_{t}")
                i2 = vector.nop()
                sf = s_x2f[b]
                i2.then_inc(sf[0], 1)
                sf[1] += 1
                vector.bn_stats(stats[:, 0:6], x2sb[sslot][:, ti * D: ti * D + 512])
                i = vector.bn_stats(stats[:, 6:12], x2sb[sslot][:, ti * D + 512:(ti + 1) * D])
                edge(vector, "s_v", i)
                i = vector.bn_aggr(mva[:, 2 * t:2 * t + 2], stats[:])
                mark(f"mv2_{t}", "s_v", i)

            def fn_srt2(scalar, label, t=t):
                wait(scalar, f"mv2_{t}")
                i = scalar.activation(out=mva[:, 2 * t + 1:2 * t + 2],
                                      in_=mva[:, 2 * t + 1:2 * t + 2],
                                      func=AF.Sqrt, scale=1.0, bias=csts[:, 1:2])
                inc_to(label, i)
            scal_defer(f"srt2_{t}", fn_srt2)

            @block.vector
            def _(vector, t=t, lb=lb, sslot=sslot, ti=ti):
                wait(vector, f"srt2_{t}")
                i = vector.reciprocal(out=rstda[:, t:t + 1], in_=mva[:, 2 * t + 1:2 * t + 2])
                mark(f"rstd2_{t}", "s_v", i)
                wait(vector, f"rstd2_{t}")
                i = vector.scalar_tensor_tensor(out=nba[:, t:t + 1],
                                                in0=rstda[:, t:t + 1], scalar=-1.0,
                                                in1=mva[:, 2 * t:2 * t + 1],
                                                op0=ALU.mult, op1=ALU.mult)
                edge(vector, "s_v", i)
                if t >= 2:
                    wait(vector, f"tr2_{t - 2}")
                i = vector.tensor_scalar(out=lnx_tm[lb][:],
                                         in0=x2sb[sslot][:, ti * D:(ti + 1) * D],
                                         scalar1=rstda[:, t:t + 1],
                                         scalar2=nba[:, t:t + 1],
                                         op0=ALU.mult, op1=ALU.add)
                mark(f"lnx2_{t}", "s_v", i)

        def b_tr2(t):
            lb = t % 2

            @block.tensor
            def _(tensor, t=t, lb=lb):
                wait(tensor, f"lnx2_{t}")
                if t >= 2:
                    wait(tensor, f"trev2_{t - 2}")
                else:
                    wait(tensor, "attnT")
                half = t % 2
                for kk in range(NKT):
                    i = tensor.transpose(
                        out=ptr[:, half * 1024 + kk * 128: half * 1024 + (kk + 1) * 128],
                        in_=lnx_tm[lb][:, kk * 128:(kk + 1) * 128],
                        identity=idents[:])
                mark(f"tr2_{t}", "s_pe", i)

            def fn_trev2(scalar, label, t=t):
                half = t % 2
                wait(scalar, f"tr2_{t}")
                i = scalar.activation(
                    out=lnx_fm[:].rearrange("p (kk tok) -> p kk tok", tok=T)[:, :, t * 128:(t + 1) * 128],
                    in_=ptr[:, half * 1024:(half + 1) * 1024].rearrange("p (kk e) -> p kk e", e=128),
                    func=AF.Copy, scale=1.0)
                inc_to(label, i)
            scal_defer(f"trev2_{t}", fn_trev2)

        def b_ffn_head(s):
            @block.tensor
            def _(tensor, s=s):
                t_last = s * ST + ST - 1
                wait(tensor, f"trev2_{t_last}")
                if s == 0:
                    wait(tensor, "ffncast1")
                    wait(tensor, "w1bd_cast")
                    wait(tensor, "w2bd_cast")
                    wait(tensor, "gones")
                else:
                    wait(tensor, f"ov_{(s - 1) * ST + ST - 1}")
                for nch in range(NCH):
                    for kk in range(NKT):
                        i = tensor.matmul(
                            out=pk[0:M_FFN, nch * 512:(nch + 1) * 512],
                            lhsT=win_b[:, kk * M_FFN:(kk + 1) * M_FFN],
                            rhs=lnx_fm[:, kk * T + s * ST * 128 + nch * 512:
                                       kk * T + s * ST * 128 + (nch + 1) * 512],
                            start=(kk == 0), stop=(kk == NKT - 1))
                mark(f"smm_{s}", "s_pe", i)

            @block.vector
            def _(vector, s=s):
                wait(vector, f"smm_{s}")
                if s == 0:
                    wait(vector, "ffnb_loaded")
                i = vector.tensor_scalar(out=s_sb[:], in0=pk[0:M_FFN, 0:ST * 128],
                                         scalar1=binc[:], scalar2=None, op0=ALU.add)
                mark(f"sev_{s}", "s_v", i)

            def ffn_h(j, s=s):
                pb_h = pqv if j % 2 == 0 else psc

                @block.tensor
                def _(tensor, s=s, j=j, pb_h=pb_h):
                    if j == 0:
                        wait(tensor, f"sev_{s}")
                        wait(tensor, f"x2_{min(s * ST + ST, NT - 2)}")
                    if j == 1:
                        wait(tensor, f"x2_{s * ST + ST - 1}")
                    if j >= 2:
                        wait(tensor, f"Hev_{s}_{j - 2}")
                    for nch in range(NCH):
                        i = tensor.matmul(
                            out=pb_h[:, nch * 512:(nch + 1) * 512],
                            lhsT=w1bd_b[:, j * 128:(j + 1) * 128],
                            rhs=s_sb[:, nch * 512:(nch + 1) * 512],
                            start=True, stop=True)
                    mark(f"Hmm_{s}_{j}", "s_pe", i)

                def fn_hev(scalar, label, s=s, j=j, pb_h=pb_h):
                    wait(scalar, f"Hmm_{s}_{j}")
                    if j >= 2:
                        wait(scalar, f"gmm_{s}_{j - 2}")
                    i = scalar.activation(out=H_sb[j % 2][:], in_=pb_h[:, 0:ST * 128],
                                          func=AF.Relu, scale=1.0, bias=b1f[:, j:j + 1])
                    inc_to(label, i)
                scal_defer(f"Hev_{s}_{j}", fn_hev)

            def ffn_g(j, s=s):
                @block.tensor
                def _(tensor, s=s, j=j):
                    wait(tensor, f"Hev_{s}_{j}")
                    for nch in range(NCH):
                        i = tensor.matmul(
                            out=pk[0:M_FFN, nch * 512:(nch + 1) * 512],
                            lhsT=w2bd_b[:, j * M_FFN:(j + 1) * M_FFN],
                            rhs=H_sb[j % 2][:, nch * 512:(nch + 1) * 512],
                            start=(j == 0), stop=(j == NJ - 1))
                    mark(f"gmm_{s}_{j}", "s_pe", i)

            for j in range(NJ):
                ffn_h(j)
                if j >= 1:
                    ffn_g(j - 1)
            ffn_g(NJ - 1)

            @block.vector
            def _(vector, s=s):
                wait(vector, f"gmm_{s}_{NJ - 1}")
                i = vector.tensor_scalar(out=gT_sb[0:M_FFN, :], in0=pk[0:M_FFN, 0:ST * 128],
                                         scalar1=b2c[:], scalar2=None, op0=ALU.add)
                mark(f"gT_{s}", "s_v", i)

        def b_fmm(s, tis):
            sslot = s % 2
            for ti in tis:
                t = s * ST + ti

                @block.tensor
                def _(tensor, s=s, ti=ti, t=t):
                    if ti == 0:
                        wait(tensor, f"gT_{s}")
                    if ti == 2 and t + 4 < NT:
                        wait(tensor, f"x2_{t + 4}")
                    pb = pqv if ti % 2 == 0 else pk
                    if t >= 2:
                        wait(tensor, f"ov_{t - 2}")
                    for nch in range(2):
                        i = tensor.matmul(
                            out=pb[:, nch * 512:(nch + 1) * 512],
                            lhsT=gT_sb[:, ti * 128:(ti + 1) * 128],
                            rhs=wout_a[:, nch * 512:(nch + 1) * 512],
                            start=True, stop=True)
                    mark(f"fmm_{t}", "s_pe", i)

                @block.vector
                def _(vector, t=t, ti=ti, sslot=sslot):
                    ob = t % 2
                    pb = pqv if ti % 2 == 0 else pk
                    wait(vector, f"fmm_{t}")
                    if t >= 2:
                        wait(vector, f"stored_{t - 2}")
                    i = vector.tensor_tensor(out=o_sb[ob][:],
                                             in0=x2sb[sslot][:, ti * D:(ti + 1) * D],
                                             in1=pb[:], op=ALU.add)
                    mark(f"ov_{t}", "s_v", i)

                @block.sync
                def _(sync, t=t):
                    ob = t % 2
                    wait(sync, f"ov_{t}")
                    i = sync.dma_start(out=out_d[t * 128:(t + 1) * 128, :], in_=o_sb[ob][:])
                    mark(f"stored_{t}", f"s_st{ob}", i, 16)

        for t in range(NT + ST):
            if t % ST == 0 and t // ST < NS:
                b_x2loads(t // ST)
            u = t - 2
            if 0 <= u < NT and u % ST != ST - 1:
                b_tr2(u)
            u1 = t - 1
            if 0 <= u1 < NT and u1 % ST == ST - 1:
                b_tr2(u1)
            if t < NT:
                b_vo(t)
                b_chain(t)
            if u1 >= 0 and u1 % ST == ST - 1:
                b_ffn_head(u1 // ST)
            if t >= 5 and (t - 5) % ST == 0 and (t - 5) // ST < NS:
                b_fmm((t - 5) // ST, [0, 1])
            if t >= 6 and (t - 6) % ST == 0 and (t - 6) // ST < NS:
                b_fmm((t - 6) // ST, [2, 3])

        scal_flush()

        @block.sync
        def _(sync):
            sync.wait_ge(sems["s_st0"][0], sems["s_st0"][1])
            sync.wait_ge(sems["s_st1"][0], sems["s_st1"][1])

    return nc


def make_const_inputs():
    ident = np.eye(128, dtype=bf16)
    eyes = (np.eye(128) * ISQ2).astype(np.float32)
    csts = np.zeros((128, 2), np.float32)
    csts[:, 1] = EPS_LN
    return ident, eyes, csts


def make_in_map(inp, core, T=4096):
    b, half = core // 2, core % 2
    x = np.ascontiguousarray(
        np.asarray(inp["L0_attn_query"], np.float32)[b, half * T:(half + 1) * T, :])
    W1 = np.asarray(inp["L0_ffn_W1"], np.float32)
    b1 = np.asarray(inp["L0_ffn_b1"], np.float32)
    W2 = np.asarray(inp["L0_ffn_W2"], np.float32)
    w1bd = np.zeros((M_FFN, M_FFN * INNER), np.float32)
    w2bd = np.zeros((M_FFN * INNER, M_FFN), np.float32)
    mi = np.arange(M_FFN)[:, None]
    ii = np.arange(INNER)[None, :]
    w1bd[mi, mi * INNER + ii] = W1
    w2bd[mi * INNER + ii, mi] = W2
    ident, eyes, csts = make_const_inputs()
    return {
        "x": x,
        "wq": np.asarray(inp["L0_attn_w_q"], np.float32),
        "wk": np.asarray(inp["L0_attn_w_k"], np.float32),
        "wv": np.asarray(inp["L0_attn_w_v"], np.float32),
        "wo": np.asarray(inp["L0_attn_w_o"], np.float32),
        "gainb": np.ascontiguousarray(np.broadcast_to(
            np.asarray(inp["L0_attn_gain"], np.float32).reshape(1, 1), (128, 1))),
        "win": np.asarray(inp["L0_ffn_Win"], np.float32),
        "binc": np.ascontiguousarray(np.asarray(inp["L0_ffn_bin_"], np.float32).reshape(M_FFN, 1)),
        "w1bd": w1bd,
        "b1f": np.ascontiguousarray(b1.reshape(M_FFN * INNER // 128, 128).T),
        "w2bd": w2bd,
        "b2c": np.ascontiguousarray(np.asarray(inp["L0_ffn_b2"], np.float32).reshape(M_FFN, 1)),
        "wout": np.asarray(inp["L0_ffn_Wout"], np.float32),
        "boutr": np.ascontiguousarray(np.asarray(inp["L0_ffn_bout"], np.float32).reshape(1, D)),
        "ident": ident, "eyes": eyes, "csts": csts,
    }


def kernel(**inputs):
    nc = build(T=4096, ST=4)
    in_maps = [make_in_map(inputs, c) for c in range(8)]
    res = run_bass_kernel_spmd(nc, in_maps, core_ids=list(range(8)))
    out = np.empty((4, 8192, 1024), np.float32)
    for c in range(8):
        b, half = c // 2, c % 2
        out[b, half * 4096:(half + 1) * 4096, :] = np.asarray(res.results[c]["out"])
    return out


# version 23
# speedup vs baseline: 1.0887x; 1.0887x over previous
import math
import contextlib
import numpy as np
import ml_dtypes
import concourse.bass as bass
import concourse.mybir as mybir
from concourse.bass_utils import run_bass_kernel_spmd

F32 = mybir.dt.float32
BF16 = mybir.dt.bfloat16
AF = mybir.ActivationFunctionType
ALU = mybir.AluOpType
AX = mybir.AxisListType
bf16 = ml_dtypes.bfloat16

D = 1024
NH = 16
HD = 64
NKT = D // 128
S_FULL = 8192
M_FFN = 64
INNER = 16
NJ = M_FFN * INNER // 128
EPS_MP = 1e-4
EPS_LN = 1e-5
ISQ2 = 1.0 / math.sqrt(2.0)


def build(T=4096, ST=4, pair_groups=None):
    NT = T // 128
    assert NT % ST == 0 and NT >= 8
    NS = NT // ST
    NCH = ST * 128 // 512
    if pair_groups is None:
        pair_groups = [[0, 1], [2, 3], [4, 5], [6, 7]]
    nc = bass.Bass("TRN2", target_bir_lowering=False, debug=False)

    x_in = nc.dram_tensor("x", [T, D], F32, kind="ExternalInput").ap()
    w_d = [nc.dram_tensor(n, [D, D], F32, kind="ExternalInput").ap()
           for n in ("wq", "wk", "wv", "wo")]
    gain_d = nc.dram_tensor("gainb", [128, 1], F32, kind="ExternalInput").ap()
    win_d = nc.dram_tensor("win", [D, M_FFN], F32, kind="ExternalInput").ap()
    bin_d = nc.dram_tensor("binc", [M_FFN, 1], F32, kind="ExternalInput").ap()
    w1bd_d = nc.dram_tensor("w1bd", [M_FFN, M_FFN * INNER], F32, kind="ExternalInput").ap()
    b1f_d = nc.dram_tensor("b1f", [128, NJ], F32, kind="ExternalInput").ap()
    w2bd_d = nc.dram_tensor("w2bd", [M_FFN * INNER, M_FFN], F32, kind="ExternalInput").ap()
    b2_d = nc.dram_tensor("b2c", [M_FFN, 1], F32, kind="ExternalInput").ap()
    wout_d = nc.dram_tensor("wout", [M_FFN, D], F32, kind="ExternalInput").ap()
    bout_d = nc.dram_tensor("boutr", [1, D], F32, kind="ExternalInput").ap()
    ident_d = nc.dram_tensor("ident", [128, 128], BF16, kind="ExternalInput").ap()
    eyes_d = nc.dram_tensor("eyes", [128, 128], F32, kind="ExternalInput").ap()
    csts_d = nc.dram_tensor("csts", [128, 2], F32, kind="ExternalInput").ap()
    out_d = nc.dram_tensor("out", [T, D], F32, kind="ExternalOutput").ap()
    cc_in = nc.dram_tensor("cc_in", [HD, NH * HD], F32)
    cc_out = nc.dram_tensor("cc_out", [HD, NH * HD], F32)
    cc_in2 = nc.dram_tensor("cc_in2", [HD, NH * HD], F32)
    cc_out2 = nc.dram_tensor("cc_out2", [HD, NH * HD], F32)

    ctx = contextlib.ExitStack()
    with ctx:
        sbt = lambda name, shape, dt: ctx.enter_context(nc.sbuf_tensor(name, shape, dt))
        pst = lambda name, shape, dt: ctx.enter_context(nc.psum_tensor(name, shape, dt))

        lnx_fm = sbt("lnx_fm", [128, NKT * T], BF16)
        wvN = sbt("wvN", [128, NKT * D], BF16)
        woT = sbt("woT", [128, NKT * D], BF16)
        MT = sbt("MT", [128, NKT * D], BF16)
        idents = sbt("idents_s", [128, 128], BF16)
        eyes = sbt("eyes_s", [128, 128], F32)
        csts = sbt("csts_s", [128, 2], F32)
        gainb = sbt("gainb_s", [128, 1], F32)
        NXB = 2
        xin = [sbt(f"xin{i}", [128, D], F32) for i in range(NXB)]
        o_sb = [sbt(f"o_sb{i}", [128, D], F32) for i in range(2)]
        lnx_tm = [sbt(f"lnx_tm{i}", [128, D], BF16) for i in range(2)]
        stats = sbt("stats", [128, 12], F32)
        mva = sbt("mva", [128, 2 * (4096 // 128)], F32)
        rstda = sbt("rstda", [128, 4096 // 128], F32)
        nba = sbt("nba", [128, 4096 // 128], F32)
        ssqqk = sbt("ssqqk", [128, 2 * NH], F32)
        lnt = sbt("lnt", [128, 2 * NH], F32)
        nrm = sbt("nrm", [128, 2 * NH], F32)
        aab = sbt("aab", [128, 2 * NH], F32)
        csc = sbt("cscl", [128, NH], F32)
        wssq = sbt("wssq", [128, NKT], F32)
        wsc = sbt("wsc", [128, 4 * NKT], F32)
        wlnt = sbt("wlnt", [128, NKT], F32)
        wnrm = sbt("wnrm", [128, NKT], F32)
        wab = sbt("wab", [128, NKT], F32)
        nmx = sbt("nmx", [HD, NH], F32)
        sums = sbt("sums", [HD, NH], F32)
        rsum = sbt("rsum", [HD, NH], F32)
        attn_b = sbt("attn_b", [HD, NH * HD], BF16)
        attnT = sbt("attnT", [128, NKT * HD], BF16)
        win_b = sbt("win_b", [128, NKT * M_FFN], BF16)
        w1bd_b = sbt("w1bd_b", [M_FFN, M_FFN * INNER], BF16)
        w2bd_b = sbt("w2bd_b", [128, NJ * M_FFN], BF16)
        wout_a = sbt("wout_a", [65, D], BF16)
        binc = sbt("binc_s", [M_FFN, 1], F32)
        b2c = sbt("b2c_s", [M_FFN, 1], F32)
        b1f = sbt("b1f_s", [128, NJ], F32)
        s_sb = sbt("s_sb", [M_FFN, ST * 128], BF16)
        gT_sb = sbt("gT_sb", [65, ST * 128], BF16)
        H_sb = [sbt(f"H_sb{i}", [128, ST * 128], BF16) for i in range(2)]
        sco = o_sb[0][0:HD, :]
        attn_e = o_sb[1][0:HD, :]

        ctx_qk = contextlib.ExitStack()
        wqT = ctx_qk.enter_context(nc.sbuf_tensor("wqT", [128, NKT * D], BF16, side="right"))
        wkT = ctx_qk.enter_context(nc.sbuf_tensor("wkT", [128, NKT * D], BF16, side="right"))

        ctx_sc = contextlib.ExitStack()
        wst = [ctx_sc.enter_context(nc.sbuf_tensor(f"wst{i}", [128, D], F32)) for i in range(2)]
        wstB = [ctx_sc.enter_context(nc.sbuf_tensor(f"wstB{i}", [128, D], F32)) for i in range(2)]
        qbf = [ctx_sc.enter_context(nc.sbuf_tensor(f"qbf{i}", [128, D], BF16)) for i in range(2)]
        wbf = qbf
        kbf = [ctx_sc.enter_context(nc.sbuf_tensor(f"kbf{i}", [128, D], BF16)) for i in range(2)]
        ktl = [ctx_sc.enter_context(nc.sbuf_tensor(f"ktl{i}", [128, D], BF16)) for i in range(2)]
        qsq = wstB[0]
        ksq = wstB[1]

        ptr = pst("ptr", [128, 2 * 1024], BF16)
        psc = pst("psc", [128, NH * HD], F32)
        ctx_ptw = contextlib.ExitStack()
        ptw = ctx_ptw.enter_context(nc.psum_tensor("ptw", [128, 2 * 1024], BF16))

        sems = {}

        def sem(name):
            if name not in sems:
                sems[name] = [ctx.enter_context(nc.semaphore(name)), 0]
            return sems[name]

        W = {}

        def mark(label, sname, instr, k=1):
            s = sem(sname)
            instr.then_inc(s[0], k)
            s[1] += k
            if label in W:
                assert W[label] == (s[0], s[1]), (label, W[label], s[1])
            W[label] = (s[0], s[1])

        def promise(label, sname, future_val):
            s = sem(sname)
            W[label] = (s[0], future_val)

        def wait(eng, label):
            h, v = W[label]
            eng.wait_ge(h, v)

        scalq = []

        def reserve(label):
            s = sem("s_sq")
            s[1] += 1
            W[label] = (s[0], s[1])

        def scal_defer(label, fn):
            reserve(label)
            scalq.append((label, fn))

        def scal_defer_raw(fn):
            scalq.append((None, fn))

        def inc_to(label, instr):
            h, v = W[label]
            instr.then_inc(h, 1)

        def scal_flush():
            items = list(scalq)
            scalq.clear()

            @block.scalar
            def _(scalar):
                for label, fn in items:
                    fn(scalar, label)

        edge_n = [0]

        def edge(eng, sname, instr):
            edge_n[0] += 1
            lbl = f"_e{edge_n[0]}"
            mark(lbl, sname, instr)
            wait(eng, lbl)

        for nme in ("s_wl0", "s_wl1", "s_wlB0", "s_wlB1", "s_cl", "s_pe", "s_v",
                    "s_a", "s_g", "s_gc", "s_cc", "s_cc2", "s_cb1", "s_cb2",
                    "s_st0", "s_st1", "s_sq"):
            sem(nme)
        s_xl = [sem(f"s_xl{i}") for i in range(NXB)]
        s_xf = [sem(f"s_xf{i}") for i in range(NXB)]
        s_x2l = [sem(f"s_x2l{i}") for i in range(NXB)]
        s_x2f = [sem(f"s_x2f{i}") for i in range(NXB)]

        block = ctx.enter_context(nc.Block())

        @block.sync
        def _(sync):
            i = sync.dma_start(out=idents[:], in_=ident_d)
            mark("c0", "s_cl", i, 16)
            i = sync.dma_start(out=eyes[:], in_=eyes_d)
            mark("c1", "s_cl", i, 16)
            i = sync.dma_start(out=csts[:], in_=csts_d)
            mark("c2", "s_cl", i, 16)
            i = sync.dma_start(out=gainb[:], in_=gain_d)
            mark("consts", "s_cl", i, 16)
            for t in range(NT):
                b = t % NXB
                if t >= NXB:
                    sync.wait_ge(s_xf[b][0], t // NXB)
                i = sync.dma_start(out=xin[b][:], in_=x_in[t * 128:(t + 1) * 128, :])
                mark(f"xload_{t}", f"s_xl{b}", i, 16)

        def wp_pass1_pool(w, stage_guard, prev_guards=None):
            base_v1 = sems["s_v"][1]
            for j in range(NKT):
                promise(f"wrs_{w}_{j}", "s_v", base_v1 + j + 1)

            @block.gpsimd
            def _(gpsimd):
                for jj in (0, 1):
                    if stage_guard is not None:
                        wait(gpsimd, stage_guard[jj])
                    i = gpsimd.dma_start(out=wst[jj][:],
                                         in_=w_d[w][jj * 128:(jj + 1) * 128, :])
                    mark(f"wld1_{w}_{jj}", f"s_wl{jj}", i, 16)
                for j in range(NKT):
                    sl = j % 2
                    wait(gpsimd, f"wld1_{w}_{j}")
                    if j >= 2:
                        wait(gpsimd, f"wrs_{w}_{j - 2}")
                    elif prev_guards is not None:
                        wait(gpsimd, prev_guards[j])
                    gpsimd.tensor_tensor(out=o_sb[sl][:], in0=wst[sl][:],
                                         in1=wst[sl][:], op=ALU.mult)
                    i = gpsimd.drain()
                    mark(f"psq_{w}_{j}", "s_gc", i)
                    if j + 2 < NKT:
                        i = gpsimd.dma_start(out=wst[sl][:],
                                             in_=w_d[w][(j + 2) * 128:(j + 3) * 128, :])
                        mark(f"wld1_{w}_{j + 2}", f"s_wl{sl}", i, 16)

            @block.vector
            def _(vector):
                for j in range(NKT):
                    wait(vector, f"psq_{w}_{j}")
                    i = vector.tensor_reduce(out=wssq[:, j:j + 1], in_=o_sb[j % 2][:],
                                             axis=AX.X, op=ALU.add)
                    mark(f"wrs_{w}_{j}", "s_v", i)
                wait(vector, f"wrs_{w}_{NKT - 1}")
                i = vector.tensor_scalar(out=wssq[:], in0=wssq[:], scalar1=1.0 / D,
                                         scalar2=None, op0=ALU.mult)
                mark(f"wrsc_{w}", "s_v", i)

        def wp_norm(w):
            def fn_wn0(scalar, label):
                wait(scalar, "consts")
                wait(scalar, f"wrsc_{w}")
                i = scalar.activation(out=wnrm[:], in_=wssq[:], func=AF.Sqrt,
                                      scale=1.0, bias=csts[:, 0:1])
                inc_to(label, i)
            scal_defer(f"wnorm_{w}", fn_wn0)

            @block.vector
            def _(vector):
                wait(vector, f"wnorm_{w}")
                vector.tensor_scalar(out=wab[:], in0=wnrm[:],
                                     scalar1=EPS_MP, scalar2=None, op0=ALU.add)
                vector.drain()
                vector.reciprocal(out=wlnt[:], in_=wab[:])
                vector.drain()
                i = vector.scalar_tensor_tensor(
                    out=wsc[:, w * NKT:(w + 1) * NKT], in0=wlnt[:],
                    scalar=1.0 / 32.0, in1=gainb[:].broadcast_to([128, NKT]),
                    op0=ALU.mult, op1=ALU.mult)
                mark(f"wsc_{w}", "s_v", i)

        def wp_pass2_vec(w, kind, dest, stage_guard, mt_guard_w=None):
            for j in range(NKT):
                sl = j % 2
                s = sem(f"s_wlB{sl}")
                s[1] += 16
                W[f"wld2_{w}_{j}"] = (s[0], s[1])

            def fn_ld01(scalar, _label, w=w):
                for jj in (0, 1):
                    i = scalar.dma_start(out=wstB[jj][:],
                                         in_=w_d[w][jj * 128:(jj + 1) * 128, :])
                    i.then_inc(W[f"wld2_{w}_{jj}"][0], 16)
            scal_defer_raw(fn_ld01)

            def mk_cast(j, w=w):
                def fn(scalar, label):
                    if j == 0:
                        wait(scalar, f"wsc_{w}")
                    wait(scalar, f"wld2_{w}_{j}")
                    tgt = (dest[:, j * D:(j + 1) * D] if kind == "N"
                           else MT[:, j * D:(j + 1) * D])
                    if kind != "N" and mt_guard_w is not None:
                        wait(scalar, f"wtr_{mt_guard_w}_{j}")
                    i = scalar.activation(
                        out=tgt, in_=wstB[j % 2][:], func=AF.Copy,
                        scale=wsc[:, w * NKT + j: w * NKT + j + 1])
                    inc_to(label, i)
                scal_defer(f"wcast_{w}_{j}", fn)
                if j + 2 < NKT:
                    def fnl(scalar, _label, j=j):
                        wait(scalar, f"wcast_{w}_{j}")
                        i = scalar.dma_start(out=wstB[j % 2][:],
                                             in_=w_d[w][(j + 2) * 128:(j + 3) * 128, :])
                        i.then_inc(W[f"wld2_{w}_{j + 2}"][0], 16)
                    scal_defer_raw(fnl)
            for j in range(NKT):
                mk_cast(j)

        def wp_post_transpose(w, dest, ptr_guard, pbuf=None):
            base_pe = sems["s_pe"][1]
            for j in range(NKT):
                promise(f"wtr_{w}_{j}", "s_pe", base_pe + j + 1)

            pb2 = ptr if pbuf is None else pbuf

            def mk_ev(j, w=w, dest=dest, pb=pb2):
                def fn(scalar, label):
                    half = j % 2
                    wait(scalar, f"wtr_{w}_{j}")
                    i = scalar.activation(
                        out=dest[:].rearrange("p (kk o) -> p kk o", o=D)[:, :, j * 128:(j + 1) * 128],
                        in_=pb[:, half * 1024:(half + 1) * 1024]
                        .rearrange("p (kk e) -> p kk e", e=128),
                        func=AF.Copy, scale=1.0)
                    inc_to(label, i)
                scal_defer(f"wtev_{w}_{j}", fn)
            for j in range(NKT):
                mk_ev(j)

            @block.tensor
            def _(tensor):
                wait(tensor, f"wcast_{w}_{NKT - 1}")
                for j in range(NKT):
                    half = j % 2
                    if j >= 2:
                        wait(tensor, f"wtev_{w}_{j - 2}")
                    elif ptr_guard is not None:
                        wait(tensor, ptr_guard)
                    pb = ptr if pbuf is None else pbuf
                    for kk in range(NKT):
                        i = tensor.transpose(
                            out=pb[:, half * 1024 + kk * 128: half * 1024 + (kk + 1) * 128],
                            in_=MT[:, j * D + kk * 128: j * D + (kk + 1) * 128],
                            identity=idents[:])
                    mark(f"wtr_{w}_{j}", "s_pe", i)

        def emit_weight_startup():
            wp_pass1_pool(0, None)
            wp_norm(0)
            wp_pass1_pool(1, None, prev_guards=["wrs_0_6", "wrs_0_7"])
            wp_pass2_vec(0, "T_viaMT", wqT, None)
            wp_post_transpose(0, wqT, None, pbuf=ptw)
            wp_norm(1)
            wp_pass2_vec(1, "T_viaMT", wkT, None, mt_guard_w=0)
            wp_post_transpose(1, wkT, "wtev_0_7", pbuf=ptw)

        ctx_ptw_closed = [False]
        pqv = None
        pk = None

        def tile_front(t):
            b = t % NXB
            lb = t % 2

            @block.vector
            def _(vector, t=t, b=b):
                h, v = W[f"xload_{t}"]
                vector.wait_ge(h, v)
                vector.bn_stats(stats[:, 0:6], xin[b][:, 0:512])
                i = vector.bn_stats(stats[:, 6:12], xin[b][:, 512:1024])
                edge(vector, "s_v", i)
                i = vector.bn_aggr(mva[:, 2 * t:2 * t + 2], stats[:])
                mark(f"mv_{t}", "s_v", i)

            def fn_srt(scalar, label, t=t):
                if t == 0:
                    wait(scalar, "consts")
                wait(scalar, f"mv_{t}")
                i = scalar.activation(out=mva[:, 2 * t + 1:2 * t + 2],
                                      in_=mva[:, 2 * t + 1:2 * t + 2],
                                      func=AF.Sqrt, scale=1.0, bias=csts[:, 1:2])
                inc_to(label, i)
            scal_defer(f"srt_{t}", fn_srt)

            @block.vector
            def _(vector, t=t, b=b, lb=lb):
                wait(vector, f"srt_{t}")
                i = vector.reciprocal(out=rstda[:, t:t + 1], in_=mva[:, 2 * t + 1:2 * t + 2])
                mark(f"rstd_{t}", "s_v", i)
                wait(vector, f"rstd_{t}")
                i = vector.scalar_tensor_tensor(out=nba[:, t:t + 1],
                                                in0=rstda[:, t:t + 1], scalar=-1.0,
                                                in1=mva[:, 2 * t:2 * t + 1],
                                                op0=ALU.mult, op1=ALU.mult)
                mark(f"nb_{t}", "s_v", i)

            def fn_lnx(scalar, label, t=t, b=b, lb=lb):
                wait(scalar, f"nb_{t}")
                if t >= 2:
                    wait(scalar, f"tr_{t - 2}")
                i = scalar.activation(out=lnx_tm[lb][:], in_=xin[b][:],
                                      func=AF.Identity,
                                      scale=rstda[:, t:t + 1],
                                      bias=nba[:, t:t + 1])
                inc_to(label, i)
                wait(scalar, label)
                i2 = scalar.nop()
                sf = s_xf[b]
                i2.then_inc(sf[0], 1)
                sf[1] += 1
            scal_defer(f"lnx_{t}", fn_lnx)

        def tile_tr(t):
            lb = t % 2

            @block.tensor
            def _(tensor, t=t, lb=lb):
                wait(tensor, f"lnx_{t}")
                if t >= 2:
                    wait(tensor, f"trev_{t - 2}")
                half = t % 2
                for kk in range(NKT):
                    i = tensor.transpose(
                        out=ptr[:, half * 1024 + kk * 128: half * 1024 + (kk + 1) * 128],
                        in_=lnx_tm[lb][:, kk * 128:(kk + 1) * 128],
                        identity=idents[:])
                mark(f"tr_{t}", "s_pe", i)

            def fn_trev(scalar, label, t=t):
                half = t % 2
                wait(scalar, f"tr_{t}")
                i = scalar.activation(
                    out=lnx_fm[:].rearrange("p (kk tok) -> p kk tok", tok=T)[:, :, t * 128:(t + 1) * 128],
                    in_=ptr[:, half * 1024:(half + 1) * 1024].rearrange("p (kk e) -> p kk e", e=128),
                    func=AF.Copy, scale=1.0)
                inc_to(label, i)
            scal_defer(f"trev_{t}", fn_trev)

        def tile_mm_a(t):
            lb = t % 2

            @block.tensor
            def _(tensor, t=t):
                wait(tensor, f"trev_{t}")
                if t == 0:
                    wait(tensor, "wtev_0_7")
                    wait(tensor, "wtev_1_7")
                if t >= 1:
                    wait(tensor, f"qbf_{t - 1}")
                for nch in range(2):
                    for kk in range(NKT):
                        i = tensor.matmul(
                            out=pqv[:, nch * 512:(nch + 1) * 512],
                            lhsT=lnx_fm[:, kk * T + t * 128: kk * T + (t + 1) * 128],
                            rhs=wqT[:, kk * D + nch * 512: kk * D + (nch + 1) * 512],
                            start=(kk == 0), stop=(kk == NKT - 1))
                mark(f"qmm_{t}", "s_pe", i)
                if t >= 1:
                    wait(tensor, f"kbf_{t - 1}")
                for nch in range(2):
                    for kk in range(NKT):
                        i = tensor.matmul(
                            out=pk[:, nch * 512:(nch + 1) * 512],
                            lhsT=lnx_fm[:, kk * T + t * 128: kk * T + (t + 1) * 128],
                            rhs=wkT[:, kk * D + nch * 512: kk * D + (nch + 1) * 512],
                            start=(kk == 0), stop=(kk == NKT - 1))
                mark(f"kmm_{t}", "s_pe", i)

            def fn_evac(scalar, label, t=t, lb=lb):
                wait(scalar, f"qmm_{t}")
                if t >= 2:
                    wait(scalar, f"scmm_{t - 2}")
                    wait(scalar, f"sq_{t - 2}")
                i = scalar.activation(out=qbf[lb][:], in_=pqv[:],
                                      func=AF.Copy, scale=0.125)
                inc_to(label, i)
            scal_defer(f"qbf_{t}", fn_evac)

            def fn_evack(scalar, label, t=t, lb=lb):
                wait(scalar, f"kmm_{t}")
                i = scalar.activation(out=kbf[lb][:], in_=pk[:],
                                      func=AF.Copy, scale=0.125)
                inc_to(label, i)
            scal_defer(f"kbf_{t}", fn_evack)

            @block.gpsimd
            def _(gpsimd, t=t, lb=lb):
                wait(gpsimd, f"qbf_{t}")
                if t >= 1:
                    wait(gpsimd, f"ssq_{t - 1}")
                gpsimd.tensor_tensor(out=qsq[:], in0=qbf[lb][:], in1=qbf[lb][:],
                                     op=ALU.mult)
                wait(gpsimd, f"kbf_{t}")
                i = gpsimd.tensor_tensor(out=ksq[:], in0=kbf[lb][:], in1=kbf[lb][:],
                                         op=ALU.mult)
                i2 = gpsimd.drain()
                mark(f"sq_{t}", "s_gc", i2)

        def tile_mm_b(t):
            lb = t % 2

            @block.vector
            def _(vector, t=t, lb=lb):
                wait(vector, f"sq_{t}")
                if t >= 1:
                    wait(vector, f"nrm_{t - 1}")
                vector.tensor_reduce(out=ssqqk[:, 0:NH],
                                     in_=qsq[:].rearrange("p (n h) -> p n h", h=HD),
                                     axis=AX.X, op=ALU.add)
                i = vector.tensor_reduce(out=ssqqk[:, NH:2 * NH],
                                         in_=ksq[:].rearrange("p (n h) -> p n h", h=HD),
                                         axis=AX.X, op=ALU.add)
                mark(f"ssq_{t}", "s_v", i)

            def fn_nrm(scalar, label, t=t):
                wait(scalar, f"ssq_{t}")
                i = scalar.activation(out=nrm[:], in_=ssqqk[:], func=AF.Sqrt,
                                      scale=1.0, bias=csts[:, 0:1])
                inc_to(label, i)
            scal_defer(f"nrm_{t}", fn_nrm)

            @block.vector
            def _(vector, t=t, lb=lb):
                wait(vector, f"nrm_{t}")
                i = vector.tensor_scalar(out=aab[:], in0=nrm[:], scalar1=EPS_MP,
                                         scalar2=None, op0=ALU.add)
                edge(vector, "s_v", i)
                i = vector.reciprocal(out=lnt[:], in_=aab[:])
                edge(vector, "s_v", i)
                i = vector.scalar_tensor_tensor(out=csc[:], in0=lnt[:, 0:NH],
                                                scalar=64.0 / math.sqrt(S_FULL),
                                                in1=lnt[:, NH:2 * NH],
                                                op0=ALU.mult, op1=ALU.mult)
                mark(f"csc_{t}", "s_v", i)
                wait(vector, f"csc_{t}")
                i = vector.tensor_tensor(
                    out=ktl[lb][:].rearrange("p (n h) -> p n h", h=HD),
                    in0=kbf[lb][:].rearrange("p (n h) -> p n h", h=HD),
                    in1=csc[:].unsqueeze(2).broadcast_to([128, NH, HD]),
                    op=ALU.mult)
                mark(f"ktl_{t}", "s_v", i)

        def tile_scores(t):
            lb = t % 2

            @block.tensor
            def _(tensor, t=t, lb=lb):
                wait(tensor, f"qbf_{t}")
                wait(tensor, f"ktl_{t}")
                if t == NT // 2:
                    wait(tensor, "sco1")
                for n in range(NH):
                    i = tensor.matmul(
                        out=psc[0:HD, n * HD:(n + 1) * HD],
                        lhsT=qbf[lb][:, n * HD:(n + 1) * HD],
                        rhs=ktl[lb][:, n * HD:(n + 1) * HD],
                        start=(t in (0, NT // 2)) and (n % 8 == 0),
                        stop=(t in (NT // 2 - 1, NT - 1)) and (n % 8 == 7),
                        skip_group_check=True)
                mark(f"scmm_{t}", "s_pe", i)

        def wp_piece_slots(w, p1_guard, kind, dest, mt_guard_w=None):
            def sq_piece(j):
                def em():
                    @block.gpsimd
                    def _(gpsimd):
                        if j == 0:
                            for jj in (0, 1):
                                if p1_guard is not None:
                                    wait(gpsimd, p1_guard[jj])
                                i = gpsimd.dma_start(
                                    out=wst[jj][:],
                                    in_=w_d[w][jj * 128:(jj + 1) * 128, :])
                                mark(f"wld1_{w}_{jj}", f"s_wl{jj}", i, 16)
                        wait(gpsimd, f"wld1_{w}_{j}")
                        if j >= 1:
                            wait(gpsimd, f"wrs_{w}_{j - 1}")
                        elif w == 2:
                            wait(gpsimd, "wrs_1_7")
                        gpsimd.tensor_tensor(out=o_sb[1][:], in0=wst[j % 2][:],
                                             in1=wst[j % 2][:], op=ALU.mult)
                        i = gpsimd.drain()
                        mark(f"psq_{w}_{j}", "s_gc", i)
                        if j + 2 < NKT:
                            i = gpsimd.dma_start(
                                out=wst[j % 2][:],
                                in_=w_d[w][(j + 2) * 128:(j + 3) * 128, :])
                            mark(f"wld1_{w}_{j + 2}", f"s_wl{j % 2}", i, 16)
                return em

            def norm_piece():
                def em():
                    @block.vector
                    def _(vector):
                        wait(vector, f"wrs_{w}_{NKT - 1}")
                        i = vector.tensor_scalar(out=wssq[:], in0=wssq[:],
                                                 scalar1=1.0 / D,
                                                 scalar2=None, op0=ALU.mult)
                        mark(f"wrsc_{w}", "s_v", i)

                    def fn_wn(scalar, label):
                        wait(scalar, f"wrsc_{w}")
                        i = scalar.activation(out=wnrm[:], in_=wssq[:],
                                              func=AF.Sqrt,
                                              scale=1.0, bias=csts[:, 0:1])
                        inc_to(label, i)
                    scal_defer(f"wnorm_{w}", fn_wn)

                    @block.vector
                    def _(vector):
                        wait(vector, f"wnorm_{w}")
                        vector.tensor_scalar(out=wab[:], in0=wnrm[:],
                                             scalar1=EPS_MP, scalar2=None,
                                             op0=ALU.add)
                        vector.drain()
                        vector.reciprocal(out=wlnt[:], in_=wab[:])
                        vector.drain()
                        i = vector.scalar_tensor_tensor(
                            out=wsc[:, w * NKT:(w + 1) * NKT], in0=wlnt[:],
                            scalar=1.0 / 32.0,
                            in1=gainb[:].broadcast_to([128, NKT]),
                            op0=ALU.mult, op1=ALU.mult)
                        mark(f"wsc_{w}", "s_v", i)
                return em

            def ld2_piece(j):
                def em():
                    @block.gpsimd
                    def _(gpsimd):
                        if j >= 2:
                            wait(gpsimd, f"wcast_{w}_{j - 2}")
                        else:
                            wait(gpsimd, f"wrs_{w}_{6 + j}")
                        i = gpsimd.dma_start(out=wst[j % 2][:],
                                             in_=w_d[w][j * 128:(j + 1) * 128, :])
                        mark(f"wld2_{w}_{j}", f"s_wl{j % 2}", i, 16)
                return em

            def cast_piece(j):
                def em():
                    def fn(scalar, label):
                        if j == 0:
                            wait(scalar, f"wsc_{w}")
                        wait(scalar, f"wld2_{w}_{j}")
                        tgt = (dest[:, j * D:(j + 1) * D] if kind == "N"
                               else MT[:, j * D:(j + 1) * D])
                        if kind != "N" and mt_guard_w is not None:
                            wait(scalar, f"wtr_{mt_guard_w}_{j}")
                        i = scalar.activation(
                            out=tgt, in_=wst[j % 2][:], func=AF.Copy,
                            scale=wsc[:, w * NKT + j: w * NKT + j + 1])
                        inc_to(label, i)
                    scal_defer(f"wcast_{w}_{j}", fn)
                return em

            def rd_piece(j):
                def em():
                    @block.vector
                    def _(vector):
                        wait(vector, f"psq_{w}_{j}")
                        i = vector.tensor_reduce(out=wssq[:, j:j + 1],
                                                 in_=o_sb[1][:],
                                                 axis=AX.X, op=ALU.add)
                        mark(f"wrs_{w}_{j}", "s_v", i)
                return em

            slots = [[sq_piece(0)]]
            for j in range(1, NKT):
                slots.append([rd_piece(j - 1), sq_piece(j)])
            slots.append([rd_piece(NKT - 1), norm_piece()])
            slots.append([ld2_piece(0), ld2_piece(1)])
            for j0 in range(0, NKT - 2, 2):
                slots.append([cast_piece(j0), ld2_piece(j0 + 2),
                              cast_piece(j0 + 1), ld2_piece(j0 + 3)])
            slots.append([cast_piece(NKT - 2), cast_piece(NKT - 1)])
            return slots

        wslots = (wp_piece_slots(2, None, "N", wvN)
                  + wp_piece_slots(3, ["wcast_2_6", "wcast_2_7"], "T_viaMT", woT,
                                   mt_guard_w=1))

        emit_weight_startup()
        ctx_ptw.close()
        pqv = pst("pqv", [128, D], F32)
        pk = pst("pk", [128, D], F32)
        tile_front(0)
        tile_front(1)
        tile_tr(0)
        tile_front(2)
        tile_tr(1)
        tile_front(3)
        tile_tr(2)
        for t in range(NT):
            if t + 4 < NT:
                tile_front(t + 4)
            if t >= 1:
                tile_mm_b(t - 1)
            if t >= 2:
                tile_scores(t - 2)
            tile_mm_a(t)
            if t + 3 < NT:
                tile_tr(t + 3)
            if 0 <= t - 2 < len(wslots):
                for em in wslots[t - 2]:
                    em()
            if t == NT // 2 + 1:
                @block.vector
                def _(vector):
                    wait(vector, f"scmm_{NT // 2 - 1}")
                    i = vector.tensor_copy(sco, psc[0:HD, :])
                    mark("sco1", "s_v", i)

                @block.sync
                def _(sync):
                    wait(sync, "sco1")
                    i = sync.dma_start(out=cc_in.ap(), in_=sco)
                    mark("cc1_in", "s_cc", i, 16)
            if t == min(29, NT - 3):
                @block.gpsimd
                def _(gpsimd):
                    wait(gpsimd, "cc1_in")
                    i = gpsimd.collective_compute(
                        "AllReduce", ALU.add, replica_groups=pair_groups,
                        ins=[cc_in.ap()], outs=[cc_out.ap()])
                    mark("cc1_done", "s_gc", i)
        for si in range(max(0, NT - 2), len(wslots)):
            for em in wslots[si]:
                em()
        tile_mm_b(NT - 1)
        tile_scores(NT - 2)
        tile_scores(NT - 1)
        wp_post_transpose(3, woT, f"trev_{NT - 1}")
        scal_flush()

        @block.gpsimd
        def _(gpsimd):
            wait(gpsimd, "wcast_3_6")
            i = gpsimd.dma_start(out=wst[0][:, 0:NKT * M_FFN]
                                 .rearrange("p (j m) -> p j m", m=M_FFN),
                                 in_=win_d.rearrange("(j p) m -> p j m", p=128))
            mark("win_loaded", "s_wl0", i, 16)
            wait(gpsimd, "wcast_3_7")
            i = gpsimd.dma_start(out=wst[1][0:M_FFN, :], in_=wout_d)
            mark("f0", "s_wl1", i, 16)
            i = gpsimd.dma_start(out=wst[1][64:65, :], in_=bout_d)
            mark("wout_loaded", "s_wl1", i, 16)
            i = gpsimd.dma_start(out=binc[:], in_=bin_d)
            mark("f1", "s_g", i, 16)
            i = gpsimd.dma_start(out=b2c[:], in_=b2_d)
            mark("f2", "s_g", i, 16)
            i = gpsimd.dma_start(out=b1f[:], in_=b1f_d)
            mark("ffnb_loaded", "s_g", i, 16)
            gpsimd.memset(gT_sb[64:65, :], 1.0)
            i = gpsimd.drain()
            mark("gones", "s_gc", i)

        @block.vector
        def _(vector):
            wait(vector, "win_loaded")
            vector.tensor_copy(win_b[:], wst[0][:, 0:NKT * M_FFN])
            wait(vector, "wout_loaded")
            vector.tensor_copy(wout_a[0:M_FFN, :], wst[1][0:M_FFN, :])
            i = vector.tensor_copy(wout_a[64:65, :], wst[1][64:65, :])
            mark("ffncast1", "s_v", i)

        @block.gpsimd
        def _(gpsimd):
            wait(gpsimd, "ffncast1")
            i = gpsimd.dma_start(out=wst[0][0:M_FFN, :], in_=w1bd_d)
            mark("w1bd_loaded", "s_wl0", i, 16)

        @block.vector
        def _(vector):
            wait(vector, "w1bd_loaded")
            i = vector.tensor_copy(w1bd_b[:], wst[0][0:M_FFN, :])
            mark("w1bd_cast", "s_v", i)

        @block.gpsimd
        def _(gpsimd):
            wait(gpsimd, "w1bd_cast")
            i = gpsimd.dma_start(
                out=wst[1][:, 0:NJ * M_FFN].rearrange("p (j m) -> p j m", m=M_FFN),
                in_=w2bd_d.rearrange("(j p) m -> p j m", p=128))
            mark("w2bd_loaded", "s_wl1", i, 16)

        @block.vector
        def _(vector):
            wait(vector, "w2bd_loaded")
            i = vector.tensor_copy(w2bd_b[:], wst[1][:, 0:NJ * M_FFN])
            mark("w2bd_cast", "s_v", i)

        @block.vector
        def _(vector):
            wait(vector, f"scmm_{NT - 1}")
            i = vector.tensor_copy(attn_e, psc[0:HD, :])
            mark("sco2", "s_v", i)

        @block.sync
        def _(sync):
            wait(sync, "sco2")
            i = sync.dma_start(out=cc_in2.ap(), in_=attn_e)
            mark("cc2_in", "s_cc2", i, 16)
            wait(sync, "cc1_done")
            i = sync.dma_start(out=sco, in_=cc_out.ap())
            mark("cc1_back", "s_cb1", i, 16)

        @block.gpsimd
        def _(gpsimd):
            wait(gpsimd, "cc2_in")
            i = gpsimd.collective_compute(
                "AllReduce", ALU.add, replica_groups=pair_groups,
                ins=[cc_in2.ap()], outs=[cc_out2.ap()])
            mark("cc2_done", "s_gc", i)

        @block.sync
        def _(sync):
            wait(sync, "cc2_done")
            i = sync.dma_start(out=attn_e, in_=cc_out2.ap())
            mark("cc2_back", "s_cb2", i, 16)

        @block.vector
        def _(vector):
            wait(vector, "cc1_back")
            wait(vector, "cc2_back")
            i = vector.tensor_tensor(out=sco, in0=sco, in1=attn_e, op=ALU.add)
            mark("scot", "s_v", i)

        @block.scalar
        def _(scalar):
            wait(scalar, "scot")
            i = scalar.activation(out=attn_e[:], in_=sco[:],
                                  func=AF.Exp, scale=1.0, bias=0.0)
            mark("attn_exp", "s_a", i)

        @block.vector
        def _(vector):
            wait(vector, "attn_exp")
            vector.tensor_reduce(out=sums[:], in_=attn_e.rearrange("p (n h) -> p n h", h=HD),
                                 axis=AX.X, op=ALU.add)
            vector.drain()
            vector.reciprocal(out=rsum[:], in_=sums[:])
            vector.drain()
            i = vector.tensor_tensor(
                out=attn_b[:].rearrange("p (n h) -> p n h", h=HD),
                in0=attn_e.rearrange("p (n h) -> p n h", h=HD),
                in1=rsum[:].unsqueeze(2).broadcast_to([HD, NH, HD]),
                op=ALU.mult)
            mark("attn_b", "s_v", i)

        @block.tensor
        def _(tensor):
            wait(tensor, "attn_b")
            wait(tensor, "wtev_3_7")
            for j in range(NKT):
                i = tensor.transpose(out=ptr[:, j * 128: j * 128 + 64],
                                     in_=attn_b[:, j * 128:(j + 1) * 128],
                                     identity=idents[0:64, 0:64])
            mark("attnT_tr", "s_pe", i)

        @block.vector
        def _(vector):
            wait(vector, "attnT_tr")
            i = vector.tensor_copy(
                attnT[:].rearrange("p (j h) -> p j h", h=HD),
                ptr[:, 0:1024].rearrange("p (j e) -> p j e", e=128)[:, :, 0:64])
            mark("attnT", "s_v", i)

        ctx_sc.close()
        A_sb = ctx.enter_context(nc.sbuf_tensor("A_sb", [128, NKT * D], BF16))

        def mk_aev(j):
            def fn(scalar, label):
                wait(scalar, f"Amm_{j}")
                scalar.activation(out=A_sb[0:64, j * D:(j + 1) * D], in_=pqv[0:64, :],
                                  func=AF.Copy, scale=1.0)
                i = scalar.activation(out=A_sb[64:128, j * D:(j + 1) * D], in_=pk[64:128, :],
                                      func=AF.Copy, scale=1.0)
                inc_to(label, i)
            scal_defer(f"Aev_{j}", fn)
        for j in range(NKT):
            mk_aev(j)

        @block.tensor
        def _(tensor):
            wait(tensor, "attnT")
            wait(tensor, "wcast_2_7")
            for j in range(NKT):
                if j >= 1:
                    wait(tensor, f"Aev_{j - 1}")
                for nch in range(2):
                    tensor.matmul(
                        out=pqv[0:64, nch * 512:(nch + 1) * 512],
                        lhsT=attnT[0:64, j * HD:(j + 1) * HD],
                        rhs=wvN[0:64, j * D + nch * 512: j * D + (nch + 1) * 512],
                        start=True, stop=True)
                    i = tensor.matmul(
                        out=pk[64:128, nch * 512:(nch + 1) * 512],
                        lhsT=attnT[64:128, j * HD:(j + 1) * HD],
                        rhs=wvN[64:128, j * D + nch * 512: j * D + (nch + 1) * 512],
                        start=True, stop=True)
                mark(f"Amm_{j}", "s_pe", i)

        base_v2 = sems["s_v"][1]
        for es in range(NKT):
            promise(f"MTev_{es}", "s_v", base_v2 + 2 * es + 2)

        @block.tensor
        def _(tensor):
            wait(tensor, f"Aev_{NKT - 1}")
            for es in range(NKT):
                if es >= 2:
                    wait(tensor, f"MTev_{es - 2}")
                pbank = pqv if es % 2 == 0 else pk
                for nch in range(2):
                    for kk in range(NKT):
                        i = tensor.matmul(
                            out=pbank[:, nch * 512:(nch + 1) * 512],
                            lhsT=A_sb[:, kk * D + es * 128: kk * D + (es + 1) * 128],
                            rhs=woT[:, kk * D + nch * 512: kk * D + (nch + 1) * 512],
                            start=(kk == 0), stop=(kk == NKT - 1))
                mark(f"MTmm_{es}", "s_pe", i)

        @block.vector
        def _(vector):
            for es in range(NKT):
                pbank = pqv if es % 2 == 0 else pk
                wait(vector, f"MTmm_{es}")
                vector.tensor_scalar(out=MT[:, es * D:(es + 1) * D], in0=pbank[:],
                                     scalar1=ISQ2, scalar2=None, op0=ALU.mult)
                s_v_ = sems["s_v"]
                i0 = vector.drain()
                i0.then_inc(s_v_[0], 1)
                s_v_[1] += 1
                i = vector.tensor_tensor(
                    out=MT[:, es * D + es * 128: es * D + (es + 1) * 128],
                    in0=MT[:, es * D + es * 128: es * D + (es + 1) * 128],
                    in1=eyes[:], op=ALU.add)
                mark(f"MTev_{es}", "s_v", i)

        ctx_qk.close()
        x2sb = [ctx.enter_context(nc.sbuf_tensor(f"x2sb{i}", [128, ST * D], F32, side="right"))
                for i in range(2)]
        xfA = [s_xf[i][1] for i in range(NXB)]

        def b_x2loads(s):
            @block.sync
            def _(sync, s=s):
                for ti in range(ST):
                    t = s * ST + ti
                    b = t % NXB
                    if t < NXB:
                        sync.wait_ge(s_xf[b][0], xfA[b])
                    else:
                        sync.wait_ge(s_x2f[b][0], t // NXB)
                    i = sync.dma_start(out=xin[b][:], in_=x_in[t * 128:(t + 1) * 128, :])
                    mark(f"x2load_{t}", f"s_x2l{b}", i, 16)

        def b_vo(t):
            s, ti = t // ST, t % ST
            pb_vo = pqv if t % 2 == 0 else psc

            @block.tensor
            def _(tensor, t=t, s=s, ti=ti, pb_vo=pb_vo):
                if t == 0:
                    wait(tensor, f"MTev_{NKT - 1}")
                if t >= 2:
                    wait(tensor, f"x2_{t - 2}")
                if t % ST == 2 and s >= 1:
                    wait(tensor, f"Hev_{s - 1}_6")
                if t % ST == 3 and s >= 1:
                    wait(tensor, f"Hev_{s - 1}_7")
                if t % ST == 0 and t >= 6:
                    wait(tensor, f"ov_{t - 6}")
                for nch in range(2):
                    for kk in range(NKT):
                        i = tensor.matmul(
                            out=pb_vo[:, nch * 512:(nch + 1) * 512],
                            lhsT=lnx_fm[:, kk * T + t * 128: kk * T + (t + 1) * 128],
                            rhs=MT[:, kk * D + nch * 512: kk * D + (nch + 1) * 512],
                            start=(kk == 0), stop=(kk == NKT - 1))
                mark(f"vo_{t}", "s_pe", i)

        def b_chain(t):
            s, ti = t // ST, t % ST
            b = t % NXB
            lb = t % 2
            sslot = s % 2
            pb_vo = pqv if t % 2 == 0 else psc

            @block.vector
            def _(vector, t=t, ti=ti, s=s, b=b, sslot=sslot, pb_vo=pb_vo):
                wait(vector, f"vo_{t}")
                h, v = W[f"x2load_{t}"]
                vector.wait_ge(h, v)
                if s >= 2 and ti == 0:
                    wait(vector, f"stored_{(s - 2) * ST + ST - 1}")
                i = vector.tensor_tensor(out=x2sb[sslot][:, ti * D:(ti + 1) * D],
                                         in0=xin[b][:], in1=pb_vo[:], op=ALU.add)
                mark(f"x2_{t}", "s_v", i)
                wait(vector, f"x2_{t}")
                i2 = vector.nop()
                sf = s_x2f[b]
                i2.then_inc(sf[0], 1)
                sf[1] += 1
                vector.bn_stats(stats[:, 0:6], x2sb[sslot][:, ti * D: ti * D + 512])
                i = vector.bn_stats(stats[:, 6:12], x2sb[sslot][:, ti * D + 512:(ti + 1) * D])
                edge(vector, "s_v", i)
                i = vector.bn_aggr(mva[:, 2 * t:2 * t + 2], stats[:])
                mark(f"mv2_{t}", "s_v", i)

            def fn_srt2(scalar, label, t=t):
                wait(scalar, f"mv2_{t}")
                i = scalar.activation(out=mva[:, 2 * t + 1:2 * t + 2],
                                      in_=mva[:, 2 * t + 1:2 * t + 2],
                                      func=AF.Sqrt, scale=1.0, bias=csts[:, 1:2])
                inc_to(label, i)
            scal_defer(f"srt2_{t}", fn_srt2)

            @block.vector
            def _(vector, t=t, lb=lb, sslot=sslot, ti=ti):
                wait(vector, f"srt2_{t}")
                i = vector.reciprocal(out=rstda[:, t:t + 1], in_=mva[:, 2 * t + 1:2 * t + 2])
                mark(f"rstd2_{t}", "s_v", i)
                wait(vector, f"rstd2_{t}")
                i = vector.scalar_tensor_tensor(out=nba[:, t:t + 1],
                                                in0=rstda[:, t:t + 1], scalar=-1.0,
                                                in1=mva[:, 2 * t:2 * t + 1],
                                                op0=ALU.mult, op1=ALU.mult)
                mark(f"nb2_{t}", "s_v", i)

            def fn_lnx2(scalar, label, t=t, lb=lb, sslot=sslot, ti=ti):
                wait(scalar, f"nb2_{t}")
                if t >= 2:
                    wait(scalar, f"tr2_{t - 2}")
                i = scalar.activation(out=lnx_tm[lb][:],
                                      in_=x2sb[sslot][:, ti * D:(ti + 1) * D],
                                      func=AF.Identity,
                                      scale=rstda[:, t:t + 1],
                                      bias=nba[:, t:t + 1])
                inc_to(label, i)
            scal_defer(f"lnx2_{t}", fn_lnx2)

        def b_tr2(t):
            lb = t % 2

            @block.tensor
            def _(tensor, t=t, lb=lb):
                wait(tensor, f"lnx2_{t}")
                if t >= 2:
                    wait(tensor, f"trev2_{t - 2}")
                else:
                    wait(tensor, "attnT")
                half = t % 2
                for kk in range(NKT):
                    i = tensor.transpose(
                        out=ptr[:, half * 1024 + kk * 128: half * 1024 + (kk + 1) * 128],
                        in_=lnx_tm[lb][:, kk * 128:(kk + 1) * 128],
                        identity=idents[:])
                mark(f"tr2_{t}", "s_pe", i)

            def fn_trev2(scalar, label, t=t):
                half = t % 2
                wait(scalar, f"tr2_{t}")
                i = scalar.activation(
                    out=lnx_fm[:].rearrange("p (kk tok) -> p kk tok", tok=T)[:, :, t * 128:(t + 1) * 128],
                    in_=ptr[:, half * 1024:(half + 1) * 1024].rearrange("p (kk e) -> p kk e", e=128),
                    func=AF.Copy, scale=1.0)
                inc_to(label, i)
            scal_defer(f"trev2_{t}", fn_trev2)

        def b_ffn_head(s):
            @block.tensor
            def _(tensor, s=s):
                t_last = s * ST + ST - 1
                wait(tensor, f"trev2_{t_last}")
                if s == 0:
                    wait(tensor, "ffncast1")
                    wait(tensor, "w1bd_cast")
                    wait(tensor, "w2bd_cast")
                    wait(tensor, "gones")
                else:
                    wait(tensor, f"ov_{(s - 1) * ST + ST - 1}")
                for nch in range(NCH):
                    for kk in range(NKT):
                        i = tensor.matmul(
                            out=pk[0:M_FFN, nch * 512:(nch + 1) * 512],
                            lhsT=win_b[:, kk * M_FFN:(kk + 1) * M_FFN],
                            rhs=lnx_fm[:, kk * T + s * ST * 128 + nch * 512:
                                       kk * T + s * ST * 128 + (nch + 1) * 512],
                            start=(kk == 0), stop=(kk == NKT - 1))
                mark(f"smm_{s}", "s_pe", i)

            @block.vector
            def _(vector, s=s):
                wait(vector, f"smm_{s}")
                if s == 0:
                    wait(vector, "ffnb_loaded")
                i = vector.tensor_scalar(out=s_sb[:], in0=pk[0:M_FFN, 0:ST * 128],
                                         scalar1=binc[:], scalar2=None, op0=ALU.add)
                mark(f"sev_{s}", "s_v", i)

            def ffn_h(j, s=s):
                pb_h = pqv if j % 2 == 0 else psc

                @block.tensor
                def _(tensor, s=s, j=j, pb_h=pb_h):
                    if j == 0:
                        wait(tensor, f"sev_{s}")
                        if s * ST + ST < NT:
                            wait(tensor, f"x2_{s * ST + ST}")
                    if j == 1 and s * ST + ST + 1 < NT:
                        wait(tensor, f"x2_{s * ST + ST + 1}")
                    if j >= 2:
                        wait(tensor, f"Hev_{s}_{j - 2}")
                    for nch in range(NCH):
                        i = tensor.matmul(
                            out=pb_h[:, nch * 512:(nch + 1) * 512],
                            lhsT=w1bd_b[:, j * 128:(j + 1) * 128],
                            rhs=s_sb[:, nch * 512:(nch + 1) * 512],
                            start=True, stop=True)
                    mark(f"Hmm_{s}_{j}", "s_pe", i)

                def fn_hev(scalar, label, s=s, j=j, pb_h=pb_h):
                    wait(scalar, f"Hmm_{s}_{j}")
                    if j >= 2:
                        wait(scalar, f"gmm_{s}_{j - 2}")
                    i = scalar.activation(out=H_sb[j % 2][:], in_=pb_h[:, 0:ST * 128],
                                          func=AF.Relu, scale=1.0, bias=b1f[:, j:j + 1])
                    inc_to(label, i)
                scal_defer(f"Hev_{s}_{j}", fn_hev)

            def ffn_g(j, s=s):
                @block.tensor
                def _(tensor, s=s, j=j):
                    wait(tensor, f"Hev_{s}_{j}")
                    for nch in range(NCH):
                        i = tensor.matmul(
                            out=pk[0:M_FFN, nch * 512:(nch + 1) * 512],
                            lhsT=w2bd_b[:, j * M_FFN:(j + 1) * M_FFN],
                            rhs=H_sb[j % 2][:, nch * 512:(nch + 1) * 512],
                            start=(j == 0), stop=(j == NJ - 1))
                    mark(f"gmm_{s}_{j}", "s_pe", i)

            for j in range(NJ):
                ffn_h(j)
                if j >= 1:
                    ffn_g(j - 1)
            ffn_g(NJ - 1)

            @block.vector
            def _(vector, s=s):
                wait(vector, f"gmm_{s}_{NJ - 1}")
                i = vector.tensor_scalar(out=gT_sb[0:M_FFN, :], in0=pk[0:M_FFN, 0:ST * 128],
                                         scalar1=b2c[:], scalar2=None, op0=ALU.add)
                mark(f"gT_{s}", "s_v", i)

        def b_fmm(s, tis):
            sslot = s % 2
            for ti in tis:
                t = s * ST + ti

                @block.tensor
                def _(tensor, s=s, ti=ti, t=t):
                    if ti == 0:
                        wait(tensor, f"gT_{s}")
                        if t + 6 < NT:
                            wait(tensor, f"x2_{t + 6}")
                    pb = pqv if ti % 2 == 0 else pk
                    if t >= 2:
                        wait(tensor, f"ov_{t - 2}")
                    for nch in range(2):
                        i = tensor.matmul(
                            out=pb[:, nch * 512:(nch + 1) * 512],
                            lhsT=gT_sb[:, ti * 128:(ti + 1) * 128],
                            rhs=wout_a[:, nch * 512:(nch + 1) * 512],
                            start=True, stop=True)
                    mark(f"fmm_{t}", "s_pe", i)

                @block.vector
                def _(vector, t=t, ti=ti, sslot=sslot):
                    ob = t % 2
                    pb = pqv if ti % 2 == 0 else pk
                    wait(vector, f"fmm_{t}")
                    if t >= 2:
                        wait(vector, f"stored_{t - 2}")
                    i = vector.tensor_tensor(out=o_sb[ob][:],
                                             in0=x2sb[sslot][:, ti * D:(ti + 1) * D],
                                             in1=pb[:], op=ALU.add)
                    mark(f"ov_{t}", "s_v", i)

                @block.sync
                def _(sync, t=t):
                    ob = t % 2
                    wait(sync, f"ov_{t}")
                    i = sync.dma_start(out=out_d[t * 128:(t + 1) * 128, :], in_=o_sb[ob][:])
                    mark(f"stored_{t}", f"s_st{ob}", i, 16)

        for t in range(NT + ST):
            if t % ST == 0 and t // ST < NS:
                b_x2loads(t // ST)
            u = t - 2
            if 0 <= u < NT:
                b_tr2(u)
            if t < NT:
                b_vo(t)
                b_chain(t)
            if u >= 0 and u % ST == ST - 1:
                b_ffn_head(u // ST)
            if t >= 6 and (t - 6) % ST == 0 and (t - 6) // ST < NS:
                b_fmm((t - 6) // ST, [0, 1])
            if t >= 7 and (t - 7) % ST == 0 and (t - 7) // ST < NS:
                b_fmm((t - 7) // ST, [2, 3])

        scal_flush()

        @block.sync
        def _(sync):
            sync.wait_ge(sems["s_st0"][0], sems["s_st0"][1])
            sync.wait_ge(sems["s_st1"][0], sems["s_st1"][1])

    return nc


def make_const_inputs():
    ident = np.eye(128, dtype=bf16)
    eyes = (np.eye(128) * ISQ2).astype(np.float32)
    csts = np.zeros((128, 2), np.float32)
    csts[:, 1] = EPS_LN
    return ident, eyes, csts


def make_in_map(inp, core, T=4096):
    b, half = core // 2, core % 2
    x = np.ascontiguousarray(
        np.asarray(inp["L0_attn_query"], np.float32)[b, half * T:(half + 1) * T, :])
    W1 = np.asarray(inp["L0_ffn_W1"], np.float32)
    b1 = np.asarray(inp["L0_ffn_b1"], np.float32)
    W2 = np.asarray(inp["L0_ffn_W2"], np.float32)
    w1bd = np.zeros((M_FFN, M_FFN * INNER), np.float32)
    w2bd = np.zeros((M_FFN * INNER, M_FFN), np.float32)
    mi = np.arange(M_FFN)[:, None]
    ii = np.arange(INNER)[None, :]
    w1bd[mi, mi * INNER + ii] = W1
    w2bd[mi * INNER + ii, mi] = W2
    ident, eyes, csts = make_const_inputs()
    return {
        "x": x,
        "wq": np.asarray(inp["L0_attn_w_q"], np.float32),
        "wk": np.asarray(inp["L0_attn_w_k"], np.float32),
        "wv": np.asarray(inp["L0_attn_w_v"], np.float32),
        "wo": np.asarray(inp["L0_attn_w_o"], np.float32),
        "gainb": np.ascontiguousarray(np.broadcast_to(
            np.asarray(inp["L0_attn_gain"], np.float32).reshape(1, 1), (128, 1))),
        "win": np.asarray(inp["L0_ffn_Win"], np.float32),
        "binc": np.ascontiguousarray(np.asarray(inp["L0_ffn_bin_"], np.float32).reshape(M_FFN, 1)),
        "w1bd": w1bd,
        "b1f": np.ascontiguousarray(b1.reshape(M_FFN * INNER // 128, 128).T),
        "w2bd": w2bd,
        "b2c": np.ascontiguousarray(np.asarray(inp["L0_ffn_b2"], np.float32).reshape(M_FFN, 1)),
        "wout": np.asarray(inp["L0_ffn_Wout"], np.float32),
        "boutr": np.ascontiguousarray(np.asarray(inp["L0_ffn_bout"], np.float32).reshape(1, D)),
        "ident": ident, "eyes": eyes, "csts": csts,
    }


def kernel(**inputs):
    nc = build(T=4096, ST=4)
    in_maps = [make_in_map(inputs, c) for c in range(8)]
    res = run_bass_kernel_spmd(nc, in_maps, core_ids=list(range(8)))
    out = np.empty((4, 8192, 1024), np.float32)
    for c in range(8):
        b, half = c // 2, c % 2
        out[b, half * 4096:(half + 1) * 4096, :] = np.asarray(res.results[c]["out"])
    return out


# version 24
# speedup vs baseline: 1.1108x; 1.0203x over previous
import math
import contextlib
import numpy as np
import ml_dtypes
import concourse.bass as bass
import concourse.mybir as mybir
from concourse.bass_utils import run_bass_kernel_spmd

F32 = mybir.dt.float32
BF16 = mybir.dt.bfloat16
AF = mybir.ActivationFunctionType
ALU = mybir.AluOpType
AX = mybir.AxisListType
bf16 = ml_dtypes.bfloat16

D = 1024
NH = 16
HD = 64
NKT = D // 128
S_FULL = 8192
M_FFN = 64
INNER = 16
NJ = M_FFN * INNER // 128
EPS_MP = 1e-4
EPS_LN = 1e-5
ISQ2 = 1.0 / math.sqrt(2.0)


def build(T=4096, ST=4, pair_groups=None):
    NT = T // 128
    assert NT % ST == 0 and NT >= 8
    NS = NT // ST
    NCH = ST * 128 // 512
    if pair_groups is None:
        pair_groups = [[0, 1], [2, 3], [4, 5], [6, 7]]
    nc = bass.Bass("TRN2", target_bir_lowering=False, debug=False)

    x_in = nc.dram_tensor("x", [T, D], F32, kind="ExternalInput").ap()
    w_d = [nc.dram_tensor(n, [D, D], F32, kind="ExternalInput").ap()
           for n in ("wq", "wk", "wv", "wo")]
    gain_d = nc.dram_tensor("gainb", [128, 1], F32, kind="ExternalInput").ap()
    win_d = nc.dram_tensor("win", [D, M_FFN], F32, kind="ExternalInput").ap()
    bin_d = nc.dram_tensor("binc", [M_FFN, 1], F32, kind="ExternalInput").ap()
    w1bd_d = nc.dram_tensor("w1bd", [M_FFN, M_FFN * INNER], F32, kind="ExternalInput").ap()
    b1f_d = nc.dram_tensor("b1f", [128, NJ], F32, kind="ExternalInput").ap()
    w2bd_d = nc.dram_tensor("w2bd", [M_FFN * INNER, M_FFN], F32, kind="ExternalInput").ap()
    b2_d = nc.dram_tensor("b2c", [M_FFN, 1], F32, kind="ExternalInput").ap()
    wout_d = nc.dram_tensor("wout", [M_FFN, D], F32, kind="ExternalInput").ap()
    bout_d = nc.dram_tensor("boutr", [1, D], F32, kind="ExternalInput").ap()
    ident_d = nc.dram_tensor("ident", [128, 128], BF16, kind="ExternalInput").ap()
    eyes_d = nc.dram_tensor("eyes", [128, 128], F32, kind="ExternalInput").ap()
    csts_d = nc.dram_tensor("csts", [128, 2], F32, kind="ExternalInput").ap()
    out_d = nc.dram_tensor("out", [T, D], F32, kind="ExternalOutput").ap()
    cc_in = nc.dram_tensor("cc_in", [HD, NH * HD], F32)
    cc_out = nc.dram_tensor("cc_out", [HD, NH * HD], F32)
    cc_in2 = nc.dram_tensor("cc_in2", [HD, NH * HD], F32)
    cc_out2 = nc.dram_tensor("cc_out2", [HD, NH * HD], F32)

    ctx = contextlib.ExitStack()
    with ctx:
        sbt = lambda name, shape, dt: ctx.enter_context(nc.sbuf_tensor(name, shape, dt))
        pst = lambda name, shape, dt: ctx.enter_context(nc.psum_tensor(name, shape, dt))

        lnx_fm = sbt("lnx_fm", [128, NKT * T], BF16)
        wvN = sbt("wvN", [128, NKT * D], BF16)
        woT = sbt("woT", [128, NKT * D], BF16)
        MT = sbt("MT", [128, NKT * D], BF16)
        idents = sbt("idents_s", [128, 128], BF16)
        eyes = sbt("eyes_s", [128, 128], F32)
        csts = sbt("csts_s", [128, 2], F32)
        gainb = sbt("gainb_s", [128, 1], F32)
        NXB = 2
        xin = [sbt(f"xin{i}", [128, D], F32) for i in range(NXB)]
        o_sb = [sbt(f"o_sb{i}", [128, D], F32) for i in range(2)]
        lnx_tm = [sbt(f"lnx_tm{i}", [128, D], BF16) for i in range(2)]
        stats = sbt("stats", [128, 12], F32)
        mva = sbt("mva", [128, 2 * (4096 // 128)], F32)
        rstda = sbt("rstda", [128, 4096 // 128], F32)
        nba = sbt("nba", [128, 4096 // 128], F32)
        ssqqk = sbt("ssqqk", [128, 2 * NH], F32)
        lnt = sbt("lnt", [128, 2 * NH], F32)
        nrm = sbt("nrm", [128, 2 * NH], F32)
        aab = sbt("aab", [128, 2 * NH], F32)
        csc = sbt("cscl", [128, NH], F32)
        wssq = sbt("wssq", [128, NKT], F32)
        wsc = sbt("wsc", [128, 4 * NKT], F32)
        wlnt = sbt("wlnt", [128, NKT], F32)
        wnrm = sbt("wnrm", [128, NKT], F32)
        wab = sbt("wab", [128, NKT], F32)
        nmx = sbt("nmx", [HD, NH], F32)
        sums = sbt("sums", [HD, NH], F32)
        rsum = sbt("rsum", [HD, NH], F32)
        attn_b = sbt("attn_b", [HD, NH * HD], BF16)
        attnT = sbt("attnT", [128, NKT * HD], BF16)
        win_b = sbt("win_b", [128, NKT * M_FFN], BF16)
        w1bd_b = sbt("w1bd_b", [M_FFN, M_FFN * INNER], BF16)
        w2bd_b = sbt("w2bd_b", [128, NJ * M_FFN], BF16)
        wout_a = sbt("wout_a", [65, D], BF16)
        binc = sbt("binc_s", [M_FFN, 1], F32)
        b2c = sbt("b2c_s", [M_FFN, 1], F32)
        b1f = sbt("b1f_s", [128, NJ], F32)
        s_sb = sbt("s_sb", [M_FFN, ST * 128], BF16)
        gT_sb = sbt("gT_sb", [65, ST * 128], BF16)
        H_sb = [sbt(f"H_sb{i}", [128, ST * 128], BF16) for i in range(2)]
        sco = o_sb[0][0:HD, :]
        attn_e = o_sb[1][0:HD, :]

        ctx_qk = contextlib.ExitStack()
        wqT = ctx_qk.enter_context(nc.sbuf_tensor("wqT", [128, NKT * D], BF16, side="right"))
        wkT = ctx_qk.enter_context(nc.sbuf_tensor("wkT", [128, NKT * D], BF16, side="right"))

        ctx_sc = contextlib.ExitStack()
        wst = [ctx_sc.enter_context(nc.sbuf_tensor(f"wst{i}", [128, D], F32)) for i in range(2)]
        wstB = [ctx_sc.enter_context(nc.sbuf_tensor(f"wstB{i}", [128, D], F32)) for i in range(2)]
        qbf = [ctx_sc.enter_context(nc.sbuf_tensor(f"qbf{i}", [128, D], BF16)) for i in range(2)]
        wbf = qbf
        kbf = [ctx_sc.enter_context(nc.sbuf_tensor(f"kbf{i}", [128, D], BF16)) for i in range(2)]
        ktl = [ctx_sc.enter_context(nc.sbuf_tensor(f"ktl{i}", [128, D], BF16)) for i in range(2)]
        qsq = wstB[0]
        ksq = wstB[1]

        ptr = pst("ptr", [128, 2 * 1024], BF16)
        psc = pst("psc", [128, NH * HD], F32)
        ctx_ptw = contextlib.ExitStack()
        ptw = ctx_ptw.enter_context(nc.psum_tensor("ptw", [128, 2 * 1024], BF16))

        sems = {}

        def sem(name):
            if name not in sems:
                sems[name] = [ctx.enter_context(nc.semaphore(name)), 0]
            return sems[name]

        W = {}

        def mark(label, sname, instr, k=1):
            s = sem(sname)
            instr.then_inc(s[0], k)
            s[1] += k
            if label in W:
                assert W[label] == (s[0], s[1]), (label, W[label], s[1])
            W[label] = (s[0], s[1])

        def promise(label, sname, future_val):
            s = sem(sname)
            W[label] = (s[0], future_val)

        def wait(eng, label):
            h, v = W[label]
            eng.wait_ge(h, v)

        scalq = []

        def reserve(label):
            s = sem("s_sq")
            s[1] += 1
            W[label] = (s[0], s[1])

        def scal_defer(label, fn):
            reserve(label)
            scalq.append((label, fn))

        def scal_defer_raw(fn):
            scalq.append((None, fn))

        def inc_to(label, instr):
            h, v = W[label]
            instr.then_inc(h, 1)

        def scal_flush():
            items = list(scalq)
            scalq.clear()

            @block.scalar
            def _(scalar):
                for label, fn in items:
                    fn(scalar, label)

        edge_n = [0]

        def edge(eng, sname, instr):
            edge_n[0] += 1
            lbl = f"_e{edge_n[0]}"
            mark(lbl, sname, instr)
            wait(eng, lbl)

        for nme in ("s_wl0", "s_wl1", "s_wlB0", "s_wlB1", "s_cl", "s_pe", "s_v",
                    "s_a", "s_g", "s_gc", "s_cc", "s_cc2", "s_cb1", "s_cb2",
                    "s_st0", "s_st1", "s_sq"):
            sem(nme)
        s_xl = [sem(f"s_xl{i}") for i in range(NXB)]
        s_xf = [sem(f"s_xf{i}") for i in range(NXB)]
        s_x2l = [sem(f"s_x2l{i}") for i in range(NXB)]
        s_x2f = [sem(f"s_x2f{i}") for i in range(NXB)]

        block = ctx.enter_context(nc.Block())

        @block.sync
        def _(sync):
            i = sync.dma_start(out=idents[:], in_=ident_d)
            mark("c0", "s_cl", i, 16)
            i = sync.dma_start(out=eyes[:], in_=eyes_d)
            mark("c1", "s_cl", i, 16)
            i = sync.dma_start(out=csts[:], in_=csts_d)
            mark("c2", "s_cl", i, 16)
            i = sync.dma_start(out=gainb[:], in_=gain_d)
            mark("consts", "s_cl", i, 16)
            for t in range(NT):
                b = t % NXB
                if t >= NXB:
                    sync.wait_ge(s_xf[b][0], t // NXB)
                i = sync.dma_start(out=xin[b][:], in_=x_in[t * 128:(t + 1) * 128, :])
                mark(f"xload_{t}", f"s_xl{b}", i, 16)

        def wp_pass1_pool(w, stage_guard, prev_guards=None):
            base_v1 = sems["s_v"][1]
            for j in range(NKT):
                promise(f"wrs_{w}_{j}", "s_v", base_v1 + j + 1)

            @block.gpsimd
            def _(gpsimd):
                for jj in (0, 1):
                    if stage_guard is not None:
                        wait(gpsimd, stage_guard[jj])
                    i = gpsimd.dma_start(out=wst[jj][:],
                                         in_=w_d[w][jj * 128:(jj + 1) * 128, :])
                    mark(f"wld1_{w}_{jj}", f"s_wl{jj}", i, 16)
                for j in range(NKT):
                    sl = j % 2
                    wait(gpsimd, f"wld1_{w}_{j}")
                    if j >= 2:
                        wait(gpsimd, f"wrs_{w}_{j - 2}")
                    elif prev_guards is not None:
                        wait(gpsimd, prev_guards[j])
                    gpsimd.tensor_tensor(out=o_sb[sl][:], in0=wst[sl][:],
                                         in1=wst[sl][:], op=ALU.mult)
                    i = gpsimd.drain()
                    mark(f"psq_{w}_{j}", "s_gc", i)
                    if j + 2 < NKT:
                        i = gpsimd.dma_start(out=wst[sl][:],
                                             in_=w_d[w][(j + 2) * 128:(j + 3) * 128, :])
                        mark(f"wld1_{w}_{j + 2}", f"s_wl{sl}", i, 16)

            @block.vector
            def _(vector):
                for j in range(NKT):
                    wait(vector, f"psq_{w}_{j}")
                    i = vector.tensor_reduce(out=wssq[:, j:j + 1], in_=o_sb[j % 2][:],
                                             axis=AX.X, op=ALU.add)
                    mark(f"wrs_{w}_{j}", "s_v", i)
                wait(vector, f"wrs_{w}_{NKT - 1}")
                i = vector.tensor_scalar(out=wssq[:], in0=wssq[:], scalar1=1.0 / D,
                                         scalar2=None, op0=ALU.mult)
                mark(f"wrsc_{w}", "s_v", i)

        def wp_norm(w):
            def fn_wn0(scalar, label):
                wait(scalar, "consts")
                wait(scalar, f"wrsc_{w}")
                i = scalar.activation(out=wnrm[:], in_=wssq[:], func=AF.Sqrt,
                                      scale=1.0, bias=csts[:, 0:1])
                inc_to(label, i)
            scal_defer(f"wnorm_{w}", fn_wn0)

            @block.vector
            def _(vector):
                wait(vector, f"wnorm_{w}")
                vector.tensor_scalar(out=wab[:], in0=wnrm[:],
                                     scalar1=EPS_MP, scalar2=None, op0=ALU.add)
                vector.drain()
                vector.reciprocal(out=wlnt[:], in_=wab[:])
                vector.drain()
                i = vector.scalar_tensor_tensor(
                    out=wsc[:, w * NKT:(w + 1) * NKT], in0=wlnt[:],
                    scalar=1.0 / 32.0, in1=gainb[:].broadcast_to([128, NKT]),
                    op0=ALU.mult, op1=ALU.mult)
                mark(f"wsc_{w}", "s_v", i)

        def wp_pass2_vec(w, kind, dest, stage_guard, mt_guard_w=None):
            for j in range(NKT):
                sl = j % 2
                s = sem(f"s_wlB{sl}")
                s[1] += 16
                W[f"wld2_{w}_{j}"] = (s[0], s[1])

            def fn_ld01(scalar, _label, w=w):
                for jj in (0, 1):
                    i = scalar.dma_start(out=wstB[jj][:],
                                         in_=w_d[w][jj * 128:(jj + 1) * 128, :])
                    i.then_inc(W[f"wld2_{w}_{jj}"][0], 16)
            scal_defer_raw(fn_ld01)

            def mk_cast(j, w=w):
                def fn(scalar, label):
                    if j == 0:
                        wait(scalar, f"wsc_{w}")
                    wait(scalar, f"wld2_{w}_{j}")
                    tgt = (dest[:, j * D:(j + 1) * D] if kind == "N"
                           else MT[:, j * D:(j + 1) * D])
                    if kind != "N" and mt_guard_w is not None:
                        wait(scalar, f"wtr_{mt_guard_w}_{j}")
                    i = scalar.activation(
                        out=tgt, in_=wstB[j % 2][:], func=AF.Copy,
                        scale=wsc[:, w * NKT + j: w * NKT + j + 1])
                    inc_to(label, i)
                scal_defer(f"wcast_{w}_{j}", fn)
                if j + 2 < NKT:
                    def fnl(scalar, _label, j=j):
                        wait(scalar, f"wcast_{w}_{j}")
                        i = scalar.dma_start(out=wstB[j % 2][:],
                                             in_=w_d[w][(j + 2) * 128:(j + 3) * 128, :])
                        i.then_inc(W[f"wld2_{w}_{j + 2}"][0], 16)
                    scal_defer_raw(fnl)
            for j in range(NKT):
                mk_cast(j)

        def wp_post_transpose(w, dest, ptr_guard, pbuf=None):
            base_pe = sems["s_pe"][1]
            for j in range(NKT):
                promise(f"wtr_{w}_{j}", "s_pe", base_pe + j + 1)

            pb2 = ptr if pbuf is None else pbuf

            def mk_ev(j, w=w, dest=dest, pb=pb2):
                def fn(scalar, label):
                    half = j % 2
                    wait(scalar, f"wtr_{w}_{j}")
                    i = scalar.activation(
                        out=dest[:].rearrange("p (kk o) -> p kk o", o=D)[:, :, j * 128:(j + 1) * 128],
                        in_=pb[:, half * 1024:(half + 1) * 1024]
                        .rearrange("p (kk e) -> p kk e", e=128),
                        func=AF.Copy, scale=1.0)
                    inc_to(label, i)
                scal_defer(f"wtev_{w}_{j}", fn)
            for j in range(NKT):
                mk_ev(j)

            @block.tensor
            def _(tensor):
                wait(tensor, f"wcast_{w}_{NKT - 1}")
                for j in range(NKT):
                    half = j % 2
                    if j >= 2:
                        wait(tensor, f"wtev_{w}_{j - 2}")
                    elif ptr_guard is not None:
                        wait(tensor, ptr_guard)
                    pb = ptr if pbuf is None else pbuf
                    for kk in range(NKT):
                        i = tensor.transpose(
                            out=pb[:, half * 1024 + kk * 128: half * 1024 + (kk + 1) * 128],
                            in_=MT[:, j * D + kk * 128: j * D + (kk + 1) * 128],
                            identity=idents[:])
                    mark(f"wtr_{w}_{j}", "s_pe", i)

        def emit_weight_startup():
            wp_pass1_pool(0, None)
            wp_norm(0)
            wp_pass1_pool(1, None, prev_guards=["wrs_0_6", "wrs_0_7"])
            wp_pass2_vec(0, "T_viaMT", wqT, None)
            wp_post_transpose(0, wqT, None, pbuf=ptw)
            wp_norm(1)
            wp_pass2_vec(1, "T_viaMT", wkT, None, mt_guard_w=0)
            wp_post_transpose(1, wkT, "wtev_0_7", pbuf=ptw)

        ctx_ptw_closed = [False]
        pqv = None
        pk = None

        def tile_front(t):
            b = t % NXB
            lb = t % 2

            @block.vector
            def _(vector, t=t, b=b):
                h, v = W[f"xload_{t}"]
                vector.wait_ge(h, v)
                vector.bn_stats(stats[:, 0:6], xin[b][:, 0:512])
                i = vector.bn_stats(stats[:, 6:12], xin[b][:, 512:1024])
                edge(vector, "s_v", i)
                i = vector.bn_aggr(mva[:, 2 * t:2 * t + 2], stats[:])
                mark(f"mv_{t}", "s_v", i)

            def fn_srt(scalar, label, t=t):
                if t == 0:
                    wait(scalar, "consts")
                wait(scalar, f"mv_{t}")
                i = scalar.activation(out=mva[:, 2 * t + 1:2 * t + 2],
                                      in_=mva[:, 2 * t + 1:2 * t + 2],
                                      func=AF.Sqrt, scale=1.0, bias=csts[:, 1:2])
                inc_to(label, i)
            scal_defer(f"srt_{t}", fn_srt)

            @block.vector
            def _(vector, t=t, b=b, lb=lb):
                wait(vector, f"srt_{t}")
                i = vector.reciprocal(out=rstda[:, t:t + 1], in_=mva[:, 2 * t + 1:2 * t + 2])
                mark(f"rstd_{t}", "s_v", i)
                wait(vector, f"rstd_{t}")
                i = vector.scalar_tensor_tensor(out=nba[:, t:t + 1],
                                                in0=rstda[:, t:t + 1], scalar=-1.0,
                                                in1=mva[:, 2 * t:2 * t + 1],
                                                op0=ALU.mult, op1=ALU.mult)
                mark(f"nb_{t}", "s_v", i)

            def fn_lnx(scalar, label, t=t, b=b, lb=lb):
                wait(scalar, f"nb_{t}")
                if t >= 2:
                    wait(scalar, f"tr_{t - 2}")
                i = scalar.activation(out=lnx_tm[lb][:], in_=xin[b][:],
                                      func=AF.Identity,
                                      scale=rstda[:, t:t + 1],
                                      bias=nba[:, t:t + 1])
                inc_to(label, i)
                wait(scalar, label)
                i2 = scalar.nop()
                sf = s_xf[b]
                i2.then_inc(sf[0], 1)
                sf[1] += 1
            scal_defer(f"lnx_{t}", fn_lnx)

        def tile_tr(t):
            lb = t % 2

            @block.tensor
            def _(tensor, t=t, lb=lb):
                wait(tensor, f"lnx_{t}")
                if t >= 2:
                    wait(tensor, f"trev_{t - 2}")
                half = t % 2
                for kk in range(NKT):
                    i = tensor.transpose(
                        out=ptr[:, half * 1024 + kk * 128: half * 1024 + (kk + 1) * 128],
                        in_=lnx_tm[lb][:, kk * 128:(kk + 1) * 128],
                        identity=idents[:])
                mark(f"tr_{t}", "s_pe", i)

            def fn_trev(scalar, label, t=t):
                half = t % 2
                wait(scalar, f"tr_{t}")
                i = scalar.activation(
                    out=lnx_fm[:].rearrange("p (kk tok) -> p kk tok", tok=T)[:, :, t * 128:(t + 1) * 128],
                    in_=ptr[:, half * 1024:(half + 1) * 1024].rearrange("p (kk e) -> p kk e", e=128),
                    func=AF.Copy, scale=1.0)
                inc_to(label, i)
            scal_defer(f"trev_{t}", fn_trev)

        def tile_mm_a(t):
            lb = t % 2

            @block.tensor
            def _(tensor, t=t):
                wait(tensor, f"trev_{t}")
                if t == 0:
                    wait(tensor, "wtev_0_7")
                    wait(tensor, "wtev_1_7")
                if t >= 1:
                    wait(tensor, f"qbf_{t - 1}")
                for nch in range(2):
                    for kk in range(NKT):
                        i = tensor.matmul(
                            out=pqv[:, nch * 512:(nch + 1) * 512],
                            lhsT=lnx_fm[:, kk * T + t * 128: kk * T + (t + 1) * 128],
                            rhs=wqT[:, kk * D + nch * 512: kk * D + (nch + 1) * 512],
                            start=(kk == 0), stop=(kk == NKT - 1))
                mark(f"qmm_{t}", "s_pe", i)
                if t >= 1:
                    wait(tensor, f"kbf_{t - 1}")
                for nch in range(2):
                    for kk in range(NKT):
                        i = tensor.matmul(
                            out=pk[:, nch * 512:(nch + 1) * 512],
                            lhsT=lnx_fm[:, kk * T + t * 128: kk * T + (t + 1) * 128],
                            rhs=wkT[:, kk * D + nch * 512: kk * D + (nch + 1) * 512],
                            start=(kk == 0), stop=(kk == NKT - 1))
                mark(f"kmm_{t}", "s_pe", i)

            def fn_evac(scalar, label, t=t, lb=lb):
                wait(scalar, f"qmm_{t}")
                if t >= 2:
                    wait(scalar, f"scmm_{t - 2}")
                    wait(scalar, f"sq_{t - 2}")
                i = scalar.activation(out=qbf[lb][:], in_=pqv[:],
                                      func=AF.Copy, scale=0.125)
                inc_to(label, i)
            scal_defer(f"qbf_{t}", fn_evac)

            def fn_evack(scalar, label, t=t, lb=lb):
                wait(scalar, f"kmm_{t}")
                i = scalar.activation(out=kbf[lb][:], in_=pk[:],
                                      func=AF.Copy, scale=0.125)
                inc_to(label, i)
            scal_defer(f"kbf_{t}", fn_evack)

            @block.gpsimd
            def _(gpsimd, t=t, lb=lb):
                wait(gpsimd, f"qbf_{t}")
                if t >= 1:
                    wait(gpsimd, f"ssq_{t - 1}")
                gpsimd.tensor_tensor(out=qsq[:], in0=qbf[lb][:], in1=qbf[lb][:],
                                     op=ALU.mult)
                wait(gpsimd, f"kbf_{t}")
                i = gpsimd.tensor_tensor(out=ksq[:], in0=kbf[lb][:], in1=kbf[lb][:],
                                         op=ALU.mult)
                i2 = gpsimd.drain()
                mark(f"sq_{t}", "s_gc", i2)

        def tile_mm_b(t):
            lb = t % 2

            @block.vector
            def _(vector, t=t, lb=lb):
                wait(vector, f"sq_{t}")
                if t >= 1:
                    wait(vector, f"nrm_{t - 1}")
                vector.tensor_reduce(out=ssqqk[:, 0:NH],
                                     in_=qsq[:].rearrange("p (n h) -> p n h", h=HD),
                                     axis=AX.X, op=ALU.add)
                i = vector.tensor_reduce(out=ssqqk[:, NH:2 * NH],
                                         in_=ksq[:].rearrange("p (n h) -> p n h", h=HD),
                                         axis=AX.X, op=ALU.add)
                mark(f"ssq_{t}", "s_v", i)

            def fn_nrm(scalar, label, t=t):
                wait(scalar, f"ssq_{t}")
                i = scalar.activation(out=nrm[:], in_=ssqqk[:], func=AF.Sqrt,
                                      scale=1.0, bias=csts[:, 0:1])
                inc_to(label, i)
            scal_defer(f"nrm_{t}", fn_nrm)

            @block.vector
            def _(vector, t=t, lb=lb):
                wait(vector, f"nrm_{t}")
                i = vector.tensor_scalar(out=aab[:], in0=nrm[:], scalar1=EPS_MP,
                                         scalar2=None, op0=ALU.add)
                edge(vector, "s_v", i)
                i = vector.reciprocal(out=lnt[:], in_=aab[:])
                edge(vector, "s_v", i)
                i = vector.scalar_tensor_tensor(out=csc[:], in0=lnt[:, 0:NH],
                                                scalar=64.0 / math.sqrt(S_FULL),
                                                in1=lnt[:, NH:2 * NH],
                                                op0=ALU.mult, op1=ALU.mult)
                mark(f"csc_{t}", "s_v", i)
                wait(vector, f"csc_{t}")
                i = vector.tensor_tensor(
                    out=ktl[lb][:].rearrange("p (n h) -> p n h", h=HD),
                    in0=kbf[lb][:].rearrange("p (n h) -> p n h", h=HD),
                    in1=csc[:].unsqueeze(2).broadcast_to([128, NH, HD]),
                    op=ALU.mult)
                mark(f"ktl_{t}", "s_v", i)

        def tile_scores(t):
            lb = t % 2

            @block.tensor
            def _(tensor, t=t, lb=lb):
                wait(tensor, f"qbf_{t}")
                wait(tensor, f"ktl_{t}")
                if t == NT // 2:
                    wait(tensor, "sco1")
                for n in range(NH):
                    i = tensor.matmul(
                        out=psc[0:HD, n * HD:(n + 1) * HD],
                        lhsT=qbf[lb][:, n * HD:(n + 1) * HD],
                        rhs=ktl[lb][:, n * HD:(n + 1) * HD],
                        start=(t in (0, NT // 2)) and (n % 8 == 0),
                        stop=(t in (NT // 2 - 1, NT - 1)) and (n % 8 == 7),
                        skip_group_check=True)
                mark(f"scmm_{t}", "s_pe", i)

        def wp_piece_slots(w, p1_guard, kind, dest, mt_guard_w=None):
            def sq_piece(j):
                def em():
                    @block.gpsimd
                    def _(gpsimd):
                        if j == 0:
                            for jj in (0, 1):
                                if p1_guard is not None:
                                    wait(gpsimd, p1_guard[jj])
                                i = gpsimd.dma_start(
                                    out=wst[jj][:],
                                    in_=w_d[w][jj * 128:(jj + 1) * 128, :])
                                mark(f"wld1_{w}_{jj}", f"s_wl{jj}", i, 16)
                        wait(gpsimd, f"wld1_{w}_{j}")
                        if j >= 1:
                            wait(gpsimd, f"wrs_{w}_{j - 1}")
                        elif w == 2:
                            wait(gpsimd, "wrs_1_7")
                        gpsimd.tensor_tensor(out=o_sb[1][:], in0=wst[j % 2][:],
                                             in1=wst[j % 2][:], op=ALU.mult)
                        i = gpsimd.drain()
                        mark(f"psq_{w}_{j}", "s_gc", i)
                        if j + 2 < NKT:
                            i = gpsimd.dma_start(
                                out=wst[j % 2][:],
                                in_=w_d[w][(j + 2) * 128:(j + 3) * 128, :])
                            mark(f"wld1_{w}_{j + 2}", f"s_wl{j % 2}", i, 16)
                return em

            def norm_piece():
                def em():
                    @block.vector
                    def _(vector):
                        wait(vector, f"wrs_{w}_{NKT - 1}")
                        i = vector.tensor_scalar(out=wssq[:], in0=wssq[:],
                                                 scalar1=1.0 / D,
                                                 scalar2=None, op0=ALU.mult)
                        mark(f"wrsc_{w}", "s_v", i)

                    def fn_wn(scalar, label):
                        wait(scalar, f"wrsc_{w}")
                        i = scalar.activation(out=wnrm[:], in_=wssq[:],
                                              func=AF.Sqrt,
                                              scale=1.0, bias=csts[:, 0:1])
                        inc_to(label, i)
                    scal_defer(f"wnorm_{w}", fn_wn)

                    @block.vector
                    def _(vector):
                        wait(vector, f"wnorm_{w}")
                        vector.tensor_scalar(out=wab[:], in0=wnrm[:],
                                             scalar1=EPS_MP, scalar2=None,
                                             op0=ALU.add)
                        vector.drain()
                        vector.reciprocal(out=wlnt[:], in_=wab[:])
                        vector.drain()
                        i = vector.scalar_tensor_tensor(
                            out=wsc[:, w * NKT:(w + 1) * NKT], in0=wlnt[:],
                            scalar=1.0 / 32.0,
                            in1=gainb[:].broadcast_to([128, NKT]),
                            op0=ALU.mult, op1=ALU.mult)
                        mark(f"wsc_{w}", "s_v", i)
                return em

            def ld2_piece(j):
                def em():
                    @block.gpsimd
                    def _(gpsimd):
                        if j >= 2:
                            wait(gpsimd, f"wcast_{w}_{j - 2}")
                        else:
                            wait(gpsimd, f"wrs_{w}_{6 + j}")
                        i = gpsimd.dma_start(out=wst[j % 2][:],
                                             in_=w_d[w][j * 128:(j + 1) * 128, :])
                        mark(f"wld2_{w}_{j}", f"s_wl{j % 2}", i, 16)
                return em

            def cast_piece(j):
                def em():
                    def fn(scalar, label):
                        if j == 0:
                            wait(scalar, f"wsc_{w}")
                        wait(scalar, f"wld2_{w}_{j}")
                        tgt = (dest[:, j * D:(j + 1) * D] if kind == "N"
                               else MT[:, j * D:(j + 1) * D])
                        if kind != "N" and mt_guard_w is not None:
                            wait(scalar, f"wtr_{mt_guard_w}_{j}")
                        i = scalar.activation(
                            out=tgt, in_=wst[j % 2][:], func=AF.Copy,
                            scale=wsc[:, w * NKT + j: w * NKT + j + 1])
                        inc_to(label, i)
                    scal_defer(f"wcast_{w}_{j}", fn)
                return em

            def rd_piece(j):
                def em():
                    @block.vector
                    def _(vector):
                        wait(vector, f"psq_{w}_{j}")
                        i = vector.tensor_reduce(out=wssq[:, j:j + 1],
                                                 in_=o_sb[1][:],
                                                 axis=AX.X, op=ALU.add)
                        mark(f"wrs_{w}_{j}", "s_v", i)
                return em

            slots = [[sq_piece(0)]]
            for j in range(1, NKT):
                slots.append([rd_piece(j - 1), sq_piece(j)])
            slots.append([rd_piece(NKT - 1), norm_piece()])
            slots.append([ld2_piece(0), ld2_piece(1)])
            for j0 in range(0, NKT - 2, 2):
                slots.append([cast_piece(j0), ld2_piece(j0 + 2),
                              cast_piece(j0 + 1), ld2_piece(j0 + 3)])
            slots.append([cast_piece(NKT - 2), cast_piece(NKT - 1)])
            return slots

        wslots = (wp_piece_slots(2, None, "N", wvN)
                  + wp_piece_slots(3, ["wcast_2_6", "wcast_2_7"], "T_viaMT", woT,
                                   mt_guard_w=1))

        emit_weight_startup()
        ctx_ptw.close()
        pqv = pst("pqv", [128, D], F32)
        pk = pst("pk", [128, D], F32)
        tile_front(0)
        tile_front(1)
        tile_tr(0)
        tile_front(2)
        tile_tr(1)
        tile_front(3)
        tile_tr(2)
        for t in range(NT):
            if t + 4 < NT:
                tile_front(t + 4)
            if t >= 1:
                tile_mm_b(t - 1)
            if t >= 2:
                tile_scores(t - 2)
            tile_mm_a(t)
            if t + 3 < NT:
                tile_tr(t + 3)
            if 0 <= t - 2 < len(wslots):
                for em in wslots[t - 2]:
                    em()
            if t == NT // 2 + 1:
                @block.vector
                def _(vector):
                    wait(vector, f"scmm_{NT // 2 - 1}")
                    i = vector.tensor_copy(sco, psc[0:HD, :])
                    mark("sco1", "s_v", i)

                @block.sync
                def _(sync):
                    wait(sync, "sco1")
                    i = sync.dma_start(out=cc_in.ap(), in_=sco)
                    mark("cc1_in", "s_cc", i, 16)
            if t == min(29, NT - 3):
                @block.gpsimd
                def _(gpsimd):
                    wait(gpsimd, "cc1_in")
                    i = gpsimd.collective_compute(
                        "AllReduce", ALU.add, replica_groups=pair_groups,
                        ins=[cc_in.ap()], outs=[cc_out.ap()])
                    mark("cc1_done", "s_gc", i)
        for si in range(max(0, NT - 2), len(wslots)):
            for em in wslots[si]:
                em()
        tile_mm_b(NT - 1)
        tile_scores(NT - 2)
        tile_scores(NT - 1)
        wp_post_transpose(3, woT, f"trev_{NT - 1}")
        scal_flush()

        @block.gpsimd
        def _(gpsimd):
            wait(gpsimd, "wcast_3_6")
            i = gpsimd.dma_start(out=wst[0][:, 0:NKT * M_FFN]
                                 .rearrange("p (j m) -> p j m", m=M_FFN),
                                 in_=win_d.rearrange("(j p) m -> p j m", p=128))
            mark("win_loaded", "s_wl0", i, 16)
            wait(gpsimd, "wcast_3_7")
            i = gpsimd.dma_start(out=wst[1][0:M_FFN, :], in_=wout_d)
            mark("f0", "s_wl1", i, 16)
            i = gpsimd.dma_start(out=wst[1][64:65, :], in_=bout_d)
            mark("wout_loaded", "s_wl1", i, 16)
            i = gpsimd.dma_start(out=binc[:], in_=bin_d)
            mark("f1", "s_g", i, 16)
            i = gpsimd.dma_start(out=b2c[:], in_=b2_d)
            mark("f2", "s_g", i, 16)
            i = gpsimd.dma_start(out=b1f[:], in_=b1f_d)
            mark("ffnb_loaded", "s_g", i, 16)
            gpsimd.memset(gT_sb[64:65, :], 1.0)
            i = gpsimd.drain()
            mark("gones", "s_gc", i)

        @block.vector
        def _(vector):
            wait(vector, "win_loaded")
            vector.tensor_copy(win_b[:], wst[0][:, 0:NKT * M_FFN])
            wait(vector, "wout_loaded")
            vector.tensor_copy(wout_a[0:M_FFN, :], wst[1][0:M_FFN, :])
            i = vector.tensor_copy(wout_a[64:65, :], wst[1][64:65, :])
            mark("ffncast1", "s_v", i)

        @block.gpsimd
        def _(gpsimd):
            wait(gpsimd, "ffncast1")
            i = gpsimd.dma_start(out=wst[0][0:M_FFN, :], in_=w1bd_d)
            mark("w1bd_loaded", "s_wl0", i, 16)

        @block.vector
        def _(vector):
            wait(vector, "w1bd_loaded")
            i = vector.tensor_copy(w1bd_b[:], wst[0][0:M_FFN, :])
            mark("w1bd_cast", "s_v", i)

        @block.gpsimd
        def _(gpsimd):
            wait(gpsimd, "w1bd_cast")
            i = gpsimd.dma_start(
                out=wst[1][:, 0:NJ * M_FFN].rearrange("p (j m) -> p j m", m=M_FFN),
                in_=w2bd_d.rearrange("(j p) m -> p j m", p=128))
            mark("w2bd_loaded", "s_wl1", i, 16)

        @block.vector
        def _(vector):
            wait(vector, "w2bd_loaded")
            i = vector.tensor_copy(w2bd_b[:], wst[1][:, 0:NJ * M_FFN])
            mark("w2bd_cast", "s_v", i)

        @block.vector
        def _(vector):
            wait(vector, f"scmm_{NT - 1}")
            i = vector.tensor_copy(attn_e, psc[0:HD, :])
            mark("sco2", "s_v", i)

        @block.sync
        def _(sync):
            wait(sync, "sco2")
            i = sync.dma_start(out=cc_in2.ap(), in_=attn_e)
            mark("cc2_in", "s_cc2", i, 16)
            wait(sync, "cc1_done")
            i = sync.dma_start(out=sco, in_=cc_out.ap())
            mark("cc1_back", "s_cb1", i, 16)

        @block.gpsimd
        def _(gpsimd):
            wait(gpsimd, "cc2_in")
            i = gpsimd.collective_compute(
                "AllReduce", ALU.add, replica_groups=pair_groups,
                ins=[cc_in2.ap()], outs=[cc_out2.ap()])
            mark("cc2_done", "s_gc", i)

        @block.sync
        def _(sync):
            wait(sync, "cc2_done")
            i = sync.dma_start(out=attn_e, in_=cc_out2.ap())
            mark("cc2_back", "s_cb2", i, 16)

        @block.vector
        def _(vector):
            wait(vector, "cc1_back")
            wait(vector, "cc2_back")
            i = vector.tensor_tensor(out=sco, in0=sco, in1=attn_e, op=ALU.add)
            mark("scot", "s_v", i)

        @block.scalar
        def _(scalar):
            wait(scalar, "scot")
            i = scalar.activation(out=attn_e[:], in_=sco[:],
                                  func=AF.Exp, scale=1.0, bias=0.0)
            mark("attn_exp", "s_a", i)

        @block.vector
        def _(vector):
            wait(vector, "attn_exp")
            vector.tensor_reduce(out=sums[:], in_=attn_e.rearrange("p (n h) -> p n h", h=HD),
                                 axis=AX.X, op=ALU.add)
            vector.drain()
            vector.reciprocal(out=rsum[:], in_=sums[:])
            vector.drain()
            i = vector.tensor_tensor(
                out=attn_b[:].rearrange("p (n h) -> p n h", h=HD),
                in0=attn_e.rearrange("p (n h) -> p n h", h=HD),
                in1=rsum[:].unsqueeze(2).broadcast_to([HD, NH, HD]),
                op=ALU.mult)
            mark("attn_b", "s_v", i)

        @block.tensor
        def _(tensor):
            wait(tensor, "attn_b")
            wait(tensor, "wtev_3_7")
            for j in range(NKT):
                i = tensor.transpose(out=ptr[:, j * 128: j * 128 + 64],
                                     in_=attn_b[:, j * 128:(j + 1) * 128],
                                     identity=idents[0:64, 0:64])
            mark("attnT_tr", "s_pe", i)

        @block.vector
        def _(vector):
            wait(vector, "attnT_tr")
            i = vector.tensor_copy(
                attnT[:].rearrange("p (j h) -> p j h", h=HD),
                ptr[:, 0:1024].rearrange("p (j e) -> p j e", e=128)[:, :, 0:64])
            mark("attnT", "s_v", i)

        ctx_sc.close()
        A_sb = ctx.enter_context(nc.sbuf_tensor("A_sb", [128, NKT * D], BF16))

        def mk_aev(j):
            def fn(scalar, label):
                wait(scalar, f"Amm_{j}")
                scalar.activation(out=A_sb[0:64, j * D:(j + 1) * D], in_=pqv[0:64, :],
                                  func=AF.Copy, scale=1.0)
                i = scalar.activation(out=A_sb[64:128, j * D:(j + 1) * D], in_=pk[64:128, :],
                                      func=AF.Copy, scale=1.0)
                inc_to(label, i)
            scal_defer(f"Aev_{j}", fn)
        for j in range(NKT):
            mk_aev(j)

        @block.tensor
        def _(tensor):
            wait(tensor, "attnT")
            wait(tensor, "wcast_2_7")
            for j in range(NKT):
                if j >= 1:
                    wait(tensor, f"Aev_{j - 1}")
                for nch in range(2):
                    tensor.matmul(
                        out=pqv[0:64, nch * 512:(nch + 1) * 512],
                        lhsT=attnT[0:64, j * HD:(j + 1) * HD],
                        rhs=wvN[0:64, j * D + nch * 512: j * D + (nch + 1) * 512],
                        start=True, stop=True)
                    i = tensor.matmul(
                        out=pk[64:128, nch * 512:(nch + 1) * 512],
                        lhsT=attnT[64:128, j * HD:(j + 1) * HD],
                        rhs=wvN[64:128, j * D + nch * 512: j * D + (nch + 1) * 512],
                        start=True, stop=True)
                mark(f"Amm_{j}", "s_pe", i)

        base_v2 = sems["s_v"][1]
        for es in range(NKT):
            promise(f"MTev_{es}", "s_v", base_v2 + 2 * es + 2)

        @block.tensor
        def _(tensor):
            wait(tensor, f"Aev_{NKT - 1}")
            for es in range(NKT):
                if es >= 2:
                    wait(tensor, f"MTev_{es - 2}")
                pbank = pqv if es % 2 == 0 else pk
                for nch in range(2):
                    for kk in range(NKT):
                        i = tensor.matmul(
                            out=pbank[:, nch * 512:(nch + 1) * 512],
                            lhsT=A_sb[:, kk * D + es * 128: kk * D + (es + 1) * 128],
                            rhs=woT[:, kk * D + nch * 512: kk * D + (nch + 1) * 512],
                            start=(kk == 0), stop=(kk == NKT - 1))
                mark(f"MTmm_{es}", "s_pe", i)

        @block.vector
        def _(vector):
            for es in range(NKT):
                pbank = pqv if es % 2 == 0 else pk
                wait(vector, f"MTmm_{es}")
                vector.tensor_scalar(out=MT[:, es * D:(es + 1) * D], in0=pbank[:],
                                     scalar1=ISQ2, scalar2=None, op0=ALU.mult)
                s_v_ = sems["s_v"]
                i0 = vector.drain()
                i0.then_inc(s_v_[0], 1)
                s_v_[1] += 1
                i = vector.tensor_tensor(
                    out=MT[:, es * D + es * 128: es * D + (es + 1) * 128],
                    in0=MT[:, es * D + es * 128: es * D + (es + 1) * 128],
                    in1=eyes[:], op=ALU.add)
                mark(f"MTev_{es}", "s_v", i)

        ctx_qk.close()
        x2sb = [ctx.enter_context(nc.sbuf_tensor(f"x2sb{i}", [128, ST * D], F32, side="right"))
                for i in range(2)]
        xfA = [s_xf[i][1] for i in range(NXB)]

        def b_x2loads(s):
            @block.sync
            def _(sync, s=s):
                for ti in range(ST):
                    t = s * ST + ti
                    b = t % NXB
                    if t < NXB:
                        sync.wait_ge(s_xf[b][0], xfA[b])
                    else:
                        sync.wait_ge(s_x2f[b][0], t // NXB)
                    i = sync.dma_start(out=xin[b][:], in_=x_in[t * 128:(t + 1) * 128, :])
                    mark(f"x2load_{t}", f"s_x2l{b}", i, 16)

        def b_vo(t):
            s, ti = t // ST, t % ST
            pb_vo = pqv if t % 2 == 0 else psc

            @block.tensor
            def _(tensor, t=t, s=s, ti=ti, pb_vo=pb_vo):
                if t == 0:
                    wait(tensor, f"MTev_{NKT - 1}")
                if t >= 2:
                    wait(tensor, f"x2_{t - 2}")
                if t % ST == 2 and s >= 1:
                    wait(tensor, f"Hev_{s - 1}_6")
                if t % ST == 3 and s >= 1:
                    wait(tensor, f"Hev_{s - 1}_7")
                if t % ST == 0 and t >= 6:
                    wait(tensor, f"ov_{t - 6}")
                for nch in range(2):
                    for kk in range(NKT):
                        i = tensor.matmul(
                            out=pb_vo[:, nch * 512:(nch + 1) * 512],
                            lhsT=lnx_fm[:, kk * T + t * 128: kk * T + (t + 1) * 128],
                            rhs=MT[:, kk * D + nch * 512: kk * D + (nch + 1) * 512],
                            start=(kk == 0), stop=(kk == NKT - 1))
                mark(f"vo_{t}", "s_pe", i)

        def b_chain(t):
            s, ti = t // ST, t % ST
            b = t % NXB
            lb = t % 2
            sslot = s % 2
            pb_vo = pqv if t % 2 == 0 else psc

            @block.vector
            def _(vector, t=t, ti=ti, s=s, b=b, sslot=sslot, pb_vo=pb_vo):
                wait(vector, f"vo_{t}")
                h, v = W[f"x2load_{t}"]
                vector.wait_ge(h, v)
                if s >= 2 and ti == 0:
                    wait(vector, f"stored_{(s - 2) * ST + ST - 1}")
                i = vector.tensor_tensor(out=x2sb[sslot][:, ti * D:(ti + 1) * D],
                                         in0=xin[b][:], in1=pb_vo[:], op=ALU.add)
                mark(f"x2_{t}", "s_v", i)
                wait(vector, f"x2_{t}")
                i2 = vector.nop()
                sf = s_x2f[b]
                i2.then_inc(sf[0], 1)
                sf[1] += 1
                vector.bn_stats(stats[:, 0:6], x2sb[sslot][:, ti * D: ti * D + 512])
                i = vector.bn_stats(stats[:, 6:12], x2sb[sslot][:, ti * D + 512:(ti + 1) * D])
                edge(vector, "s_v", i)
                i = vector.bn_aggr(mva[:, 2 * t:2 * t + 2], stats[:])
                mark(f"mv2_{t}", "s_v", i)

            def fn_srt2(scalar, label, t=t):
                wait(scalar, f"mv2_{t}")
                i = scalar.activation(out=mva[:, 2 * t + 1:2 * t + 2],
                                      in_=mva[:, 2 * t + 1:2 * t + 2],
                                      func=AF.Sqrt, scale=1.0, bias=csts[:, 1:2])
                inc_to(label, i)
            scal_defer(f"srt2_{t}", fn_srt2)

            @block.vector
            def _(vector, t=t, lb=lb, sslot=sslot, ti=ti):
                wait(vector, f"srt2_{t}")
                i = vector.reciprocal(out=rstda[:, t:t + 1], in_=mva[:, 2 * t + 1:2 * t + 2])
                mark(f"rstd2_{t}", "s_v", i)
                wait(vector, f"rstd2_{t}")
                i = vector.scalar_tensor_tensor(out=nba[:, t:t + 1],
                                                in0=rstda[:, t:t + 1], scalar=-1.0,
                                                in1=mva[:, 2 * t:2 * t + 1],
                                                op0=ALU.mult, op1=ALU.mult)
                mark(f"nb2_{t}", "s_v", i)

            def fn_lnx2(scalar, label, t=t, lb=lb, sslot=sslot, ti=ti):
                wait(scalar, f"nb2_{t}")
                if t >= 2:
                    wait(scalar, f"tr2_{t - 2}")
                i = scalar.activation(out=lnx_tm[lb][:],
                                      in_=x2sb[sslot][:, ti * D:(ti + 1) * D],
                                      func=AF.Identity,
                                      scale=rstda[:, t:t + 1],
                                      bias=nba[:, t:t + 1])
                inc_to(label, i)
            scal_defer(f"lnx2_{t}", fn_lnx2)

        def b_tr2(t):
            lb = t % 2

            @block.tensor
            def _(tensor, t=t, lb=lb):
                wait(tensor, f"lnx2_{t}")
                if t >= 2:
                    wait(tensor, f"trev2_{t - 2}")
                else:
                    wait(tensor, "attnT")
                half = t % 2
                for kk in range(NKT):
                    i = tensor.transpose(
                        out=ptr[:, half * 1024 + kk * 128: half * 1024 + (kk + 1) * 128],
                        in_=lnx_tm[lb][:, kk * 128:(kk + 1) * 128],
                        identity=idents[:])
                mark(f"tr2_{t}", "s_pe", i)

            def fn_trev2(scalar, label, t=t):
                half = t % 2
                wait(scalar, f"tr2_{t}")
                i = scalar.activation(
                    out=lnx_fm[:].rearrange("p (kk tok) -> p kk tok", tok=T)[:, :, t * 128:(t + 1) * 128],
                    in_=ptr[:, half * 1024:(half + 1) * 1024].rearrange("p (kk e) -> p kk e", e=128),
                    func=AF.Copy, scale=1.0)
                inc_to(label, i)
            scal_defer(f"trev2_{t}", fn_trev2)

        def b_ffn_head(s):
            @block.tensor
            def _(tensor, s=s):
                t_last = s * ST + ST - 1
                wait(tensor, f"trev2_{t_last}")
                if s == 0:
                    wait(tensor, "ffncast1")
                    wait(tensor, "w1bd_cast")
                    wait(tensor, "w2bd_cast")
                    wait(tensor, "gones")
                else:
                    wait(tensor, f"ov_{(s - 1) * ST + ST - 1}")
                for nch in range(NCH):
                    for kk in range(NKT):
                        i = tensor.matmul(
                            out=pk[0:M_FFN, nch * 512:(nch + 1) * 512],
                            lhsT=win_b[:, kk * M_FFN:(kk + 1) * M_FFN],
                            rhs=lnx_fm[:, kk * T + s * ST * 128 + nch * 512:
                                       kk * T + s * ST * 128 + (nch + 1) * 512],
                            start=(kk == 0), stop=(kk == NKT - 1))
                mark(f"smm_{s}", "s_pe", i)

            def fn_sev(scalar, label, s=s):
                wait(scalar, f"smm_{s}")
                if s == 0:
                    wait(scalar, "ffnb_loaded")
                i = scalar.activation(out=s_sb[:], in_=pk[0:M_FFN, 0:ST * 128],
                                      func=AF.Identity, scale=1.0, bias=binc[:])
                inc_to(label, i)
            scal_defer(f"sev_{s}", fn_sev)

            def ffn_h(j, s=s):
                pb_h = pqv if j % 2 == 0 else psc

                @block.tensor
                def _(tensor, s=s, j=j, pb_h=pb_h):
                    if j == 0:
                        wait(tensor, f"sev_{s}")
                        if s * ST + ST < NT:
                            wait(tensor, f"x2_{s * ST + ST}")
                    if j == 1 and s * ST + ST + 1 < NT:
                        wait(tensor, f"x2_{s * ST + ST + 1}")
                    if j >= 2:
                        wait(tensor, f"Hev_{s}_{j - 2}")
                    for nch in range(NCH):
                        i = tensor.matmul(
                            out=pb_h[:, nch * 512:(nch + 1) * 512],
                            lhsT=w1bd_b[:, j * 128:(j + 1) * 128],
                            rhs=s_sb[:, nch * 512:(nch + 1) * 512],
                            start=True, stop=True)
                    mark(f"Hmm_{s}_{j}", "s_pe", i)

                def fn_hev(scalar, label, s=s, j=j, pb_h=pb_h):
                    wait(scalar, f"Hmm_{s}_{j}")
                    if j >= 2:
                        wait(scalar, f"gmm_{s}_{j - 2}")
                    i = scalar.activation(out=H_sb[j % 2][:], in_=pb_h[:, 0:ST * 128],
                                          func=AF.Relu, scale=1.0, bias=b1f[:, j:j + 1])
                    inc_to(label, i)
                scal_defer(f"Hev_{s}_{j}", fn_hev)

            def ffn_g(j, s=s):
                @block.tensor
                def _(tensor, s=s, j=j):
                    wait(tensor, f"Hev_{s}_{j}")
                    for nch in range(NCH):
                        i = tensor.matmul(
                            out=pk[0:M_FFN, nch * 512:(nch + 1) * 512],
                            lhsT=w2bd_b[:, j * M_FFN:(j + 1) * M_FFN],
                            rhs=H_sb[j % 2][:, nch * 512:(nch + 1) * 512],
                            start=(j == 0), stop=(j == NJ - 1))
                    mark(f"gmm_{s}_{j}", "s_pe", i)

            for j in range(NJ):
                ffn_h(j)
                if j >= 1:
                    ffn_g(j - 1)
            ffn_g(NJ - 1)

            def fn_gT(scalar, label, s=s):
                wait(scalar, f"gmm_{s}_{NJ - 1}")
                i = scalar.activation(out=gT_sb[0:M_FFN, :], in_=pk[0:M_FFN, 0:ST * 128],
                                      func=AF.Identity, scale=1.0, bias=b2c[:])
                inc_to(label, i)
            scal_defer(f"gT_{s}", fn_gT)

        def b_fmm(s, tis):
            sslot = s % 2
            for ti in tis:
                t = s * ST + ti

                @block.tensor
                def _(tensor, s=s, ti=ti, t=t):
                    if ti == 0:
                        wait(tensor, f"gT_{s}")
                        if t + 6 < NT:
                            wait(tensor, f"x2_{t + 6}")
                    pb = pqv if ti % 2 == 0 else pk
                    if t >= 2:
                        wait(tensor, f"ov_{t - 2}")
                    for nch in range(2):
                        i = tensor.matmul(
                            out=pb[:, nch * 512:(nch + 1) * 512],
                            lhsT=gT_sb[:, ti * 128:(ti + 1) * 128],
                            rhs=wout_a[:, nch * 512:(nch + 1) * 512],
                            start=True, stop=True)
                    mark(f"fmm_{t}", "s_pe", i)

                @block.vector
                def _(vector, t=t, ti=ti, sslot=sslot):
                    ob = t % 2
                    pb = pqv if ti % 2 == 0 else pk
                    wait(vector, f"fmm_{t}")
                    if t >= 2:
                        wait(vector, f"stored_{t - 2}")
                    i = vector.tensor_tensor(out=o_sb[ob][:],
                                             in0=x2sb[sslot][:, ti * D:(ti + 1) * D],
                                             in1=pb[:], op=ALU.add)
                    mark(f"ov_{t}", "s_v", i)

                @block.sync
                def _(sync, t=t):
                    ob = t % 2
                    wait(sync, f"ov_{t}")
                    i = sync.dma_start(out=out_d[t * 128:(t + 1) * 128, :], in_=o_sb[ob][:])
                    mark(f"stored_{t}", f"s_st{ob}", i, 16)

        for t in range(NT + ST):
            if t % ST == 0 and t // ST < NS:
                b_x2loads(t // ST)
            u = t - 2
            if 0 <= u < NT:
                b_tr2(u)
            if t < NT:
                b_vo(t)
                b_chain(t)
            if u >= 0 and u % ST == ST - 1:
                b_ffn_head(u // ST)
            if t >= 6 and (t - 6) % ST == 0 and (t - 6) // ST < NS:
                b_fmm((t - 6) // ST, [0, 1])
            if t >= 7 and (t - 7) % ST == 0 and (t - 7) // ST < NS:
                b_fmm((t - 7) // ST, [2, 3])

        scal_flush()

        @block.sync
        def _(sync):
            sync.wait_ge(sems["s_st0"][0], sems["s_st0"][1])
            sync.wait_ge(sems["s_st1"][0], sems["s_st1"][1])

    return nc


def make_const_inputs():
    ident = np.eye(128, dtype=bf16)
    eyes = (np.eye(128) * ISQ2).astype(np.float32)
    csts = np.zeros((128, 2), np.float32)
    csts[:, 1] = EPS_LN
    return ident, eyes, csts


def make_in_map(inp, core, T=4096):
    b, half = core // 2, core % 2
    x = np.ascontiguousarray(
        np.asarray(inp["L0_attn_query"], np.float32)[b, half * T:(half + 1) * T, :])
    W1 = np.asarray(inp["L0_ffn_W1"], np.float32)
    b1 = np.asarray(inp["L0_ffn_b1"], np.float32)
    W2 = np.asarray(inp["L0_ffn_W2"], np.float32)
    w1bd = np.zeros((M_FFN, M_FFN * INNER), np.float32)
    w2bd = np.zeros((M_FFN * INNER, M_FFN), np.float32)
    mi = np.arange(M_FFN)[:, None]
    ii = np.arange(INNER)[None, :]
    w1bd[mi, mi * INNER + ii] = W1
    w2bd[mi * INNER + ii, mi] = W2
    ident, eyes, csts = make_const_inputs()
    return {
        "x": x,
        "wq": np.asarray(inp["L0_attn_w_q"], np.float32),
        "wk": np.asarray(inp["L0_attn_w_k"], np.float32),
        "wv": np.asarray(inp["L0_attn_w_v"], np.float32),
        "wo": np.asarray(inp["L0_attn_w_o"], np.float32),
        "gainb": np.ascontiguousarray(np.broadcast_to(
            np.asarray(inp["L0_attn_gain"], np.float32).reshape(1, 1), (128, 1))),
        "win": np.asarray(inp["L0_ffn_Win"], np.float32),
        "binc": np.ascontiguousarray(np.asarray(inp["L0_ffn_bin_"], np.float32).reshape(M_FFN, 1)),
        "w1bd": w1bd,
        "b1f": np.ascontiguousarray(b1.reshape(M_FFN * INNER // 128, 128).T),
        "w2bd": w2bd,
        "b2c": np.ascontiguousarray(np.asarray(inp["L0_ffn_b2"], np.float32).reshape(M_FFN, 1)),
        "wout": np.asarray(inp["L0_ffn_Wout"], np.float32),
        "boutr": np.ascontiguousarray(np.asarray(inp["L0_ffn_bout"], np.float32).reshape(1, D)),
        "ident": ident, "eyes": eyes, "csts": csts,
    }


def kernel(**inputs):
    nc = build(T=4096, ST=4)
    in_maps = [make_in_map(inputs, c) for c in range(8)]
    res = run_bass_kernel_spmd(nc, in_maps, core_ids=list(range(8)))
    out = np.empty((4, 8192, 1024), np.float32)
    for c in range(8):
        b, half = c // 2, c % 2
        out[b, half * 4096:(half + 1) * 4096, :] = np.asarray(res.results[c]["out"])
    return out
